# version 19
# baseline (speedup 1.0000x reference)
import contextlib
import numpy as np
import ml_dtypes

import concourse.bass as bass
import concourse.mybir as mybir

F32 = mybir.dt.float32
BF16 = mybir.dt.bfloat16
AF = mybir.ActivationFunctionType
ALU = mybir.AluOpType
AX = mybir.AxisListType

SEQ = 4096
DM = 1024
NT = SEQ // 128
NCH = SEQ // 512
KD = DM // 128


class Sem:
    def __init__(self, st, nc, name):
        self.h = st.enter_context(nc.semaphore(name))
        self.v = 0

    def inc(self, instr, n=1):
        instr.then_inc(self.h, n)
        self.v += n
        return self.v


def wait(eng, sem, val=None):
    v = sem.v if val is None else val
    assert v <= sem.v, f"forward wait {v} > {sem.v}"
    if v > 0:
        eng.wait_ge(sem.h, v)


def wait_f(eng, sem, val):
    if val > 0:
        eng.wait_ge(sem.h, val)


class Ctx:
    pass


def host_consts():
    c = {}
    c["IDBF"] = np.eye(128, dtype=ml_dtypes.bfloat16)
    c["HSELT"] = np.stack([np.arange(128) < 64, np.arange(128) >= 64]).astype(np.float32)
    c["HSEL"] = np.ascontiguousarray(c["HSELT"].T)
    c["IDF32"] = np.eye(128, dtype=np.float32)
    jj, ii = np.meshgrid(np.arange(128), np.arange(128), indexing="ij")
    c["TRIT"] = (jj <= ii).astype(ml_dtypes.bfloat16)
    t = np.arange(SEQ, dtype=np.float64)
    cgrid = np.arange(32, dtype=np.float64)
    ts = 10000.0 ** (2.0 * cgrid / 64.0)
    ang = t[:, None] / ts[None, :]
    cosv = np.cos(ang).astype(np.float32).reshape(NT, 128, 32).transpose(1, 0, 2)
    sinv = np.sin(ang).astype(np.float32).reshape(NT, 128, 32).transpose(1, 0, 2)
    c["ROPE_COS"] = np.ascontiguousarray(cosv)
    c["ROPE_SIN"] = np.ascontiguousarray(sinv)
    return c


def common_setup(ctx, st, nc, want=("IDBF", "HSEL", "HSELT"), slim=False, nz=2,
                 alloc_zsq=True, alloc_xbf=True, pfx="", cdram=None):
    ctx.nc = nc
    ctx.st = st
    ctx.pfx = pfx

    def SB(name, shape, dt):
        return st.enter_context(nc.sbuf_tensor(pfx + name, shape, dt))
    ctx.SB = SB
    ctx.sIN = Sem(st, nc, pfx + "sIN")
    ctx.sPE = Sem(st, nc, pfx + "sPE")
    ctx.sVE = Sem(st, nc, pfx + "sVE")
    ctx.sSC = Sem(st, nc, pfx + "sSC")
    ctx.sGP = Sem(st, nc, pfx + "sGP")
    ctx.sST = Sem(st, nc, pfx + "sST")
    ctx.p0 = st.enter_context(nc.psum_tensor(pfx + "p0", [128, 512], F32))
    ctx.p1 = st.enter_context(nc.psum_tensor(pfx + "p1", [128, 512], F32))
    ctx.p2 = st.enter_context(nc.psum_tensor(pfx + "p2", [128, 512], F32))
    ctx.p3 = st.enter_context(nc.psum_tensor(pfx + "p3", [128, 512], F32))
    ctx.p45 = st.enter_context(nc.psum_tensor(pfx + "p45", [128, 1024], F32))
    ctx.p6 = st.enter_context(nc.psum_tensor(pfx + "p6", [128, 1024], BF16))
    ctx.p7 = st.enter_context(nc.psum_tensor(pfx + "p7", [128, 512], F32))
    ctx.dram = {}

    def const_in(name, shape, dt):
        if cdram is not None:
            d = cdram[name]
        else:
            d = nc.dram_tensor(name, shape, dt, kind="ExternalInput").ap()
        sb = st.enter_context(nc.sbuf_tensor(pfx + name.lower() + "_sb", shape, dt))
        ctx.dram[name] = (sb, d)
        return sb
    if "IDBF" in want:
        ctx.idbf = const_in("IDBF", [128, 128], BF16)
    if "HSEL" in want:
        ctx.hsel = const_in("HSEL", [128, 2], F32)
    if "HSELT" in want:
        ctx.hselt = const_in("HSELT", [2, 128], F32)
    if "TRIT" in want:
        ctx.trit = const_in("TRIT", [128, 128], BF16)
    if "ROPE_COS" in want:
        ctx.ropec = const_in("ROPE_COS", [128, NT, 32], F32)
        ctx.ropes = const_in("ROPE_SIN", [128, NT, 32], F32)
    if "IDF32" in want:
        ctx.idf32 = const_in("IDF32", [128, 128], F32)
    ctx.cneg16 = st.enter_context(nc.sbuf_tensor(pfx + "cneg16", [128, 1], F32))
    ctx.cone = st.enter_context(nc.sbuf_tensor(pfx + "cone", [128, 1], F32))
    ctx.ceps = st.enter_context(nc.sbuf_tensor(pfx + "ceps", [128, 1], F32))
    ctx.ones64 = st.enter_context(nc.sbuf_tensor(pfx + "ones64", [1, 64], F32))
    ctx.ones128 = st.enter_context(nc.sbuf_tensor(pfx + "ones128", [1, 128], F32))
    ctx.nring = 2 if slim else 3
    ctx.nres = 1 if slim else 3
    ctx.nout = 1 if slim else 2
    ctx.xin = st.enter_context(nc.sbuf_tensor(pfx + "xin", [128, ctx.nring, 1024], F32))
    ctx.xres = st.enter_context(nc.sbuf_tensor(pfx + "xres", [128, ctx.nres, 1024], F32))
    if alloc_xbf:
        ctx.xbf = st.enter_context(nc.sbuf_tensor(pfx + "xbf", [128, 2, 1024], BF16))
    ctx.nz = nz
    ctx.z_sb = st.enter_context(nc.sbuf_tensor(pfx + "z_sb", [128, nz, 1024], F32))
    if alloc_zsq:
        ctx.zsq = st.enter_context(nc.sbuf_tensor(pfx + "zsq", [128, 1024], F32))
    ctx.x1o = st.enter_context(nc.sbuf_tensor(pfx + "x1o", [128, ctx.nout, 1024], F32))
    ctx.lnst = st.enter_context(nc.sbuf_tensor(pfx + "lnst", [128, 8], F32))


def emit_const_loads(ctx, block):
    @block.sync
    def _(sync):
        for name, (sb, d) in ctx.dram.items():
            ctx.sIN.inc(sync.dma_start(out=sb[:], in_=d), 16)

    @block.vector
    def _(v):
        v.memset(ctx.cneg16[:], -16.0)
        v.memset(ctx.cone[:], 1.0)
        v.memset(ctx.ceps[:], 1e-5)
        v.memset(ctx.ones64[:], 1.0)
        ctx.sVE.inc(v.memset(ctx.ones128[:], 1.0))
    ctx.const_ready_ve = ctx.sVE.v
    ctx.const_ready_in = ctx.sIN.v


def emit_weight_load(ctx, block, dst_bf, src_dram, ncols, ktiles, ring_base=0):
    for k in range(ktiles):
        @block.sync
        def _(sync):
            wait(sync, ctx.sVE)
            ctx.sIN.inc(sync.dma_start(
                out=ctx.xin[:, (ring_base + k) % ctx.nring, 0:ncols],
                in_=src_dram[k * 128:(k + 1) * 128, 0:ncols]), 16)
        ld = ctx.sIN.v

        @block.vector
        def _(v):
            wait(v, ctx.sIN, ld)
            ctx.sVE.inc(v.tensor_copy(out=dst_bf[:, k, :],
                                      in_=ctx.xin[:, (ring_base + k) % ctx.nring, 0:ncols]))


def emit_tokmaj_to_dmajor(ctx, block, src_ap, dst_bf, tile_idx, src_is_bf=False,
                          evac_extra_waits=()):
    slot = tile_idx % 2
    if not src_is_bf:
        @block.vector
        def _(v):
            wait(v, ctx.sPE)
            ctx.sVE.inc(v.tensor_copy(out=ctx.xbf[:, slot, :], in_=src_ap))
        src = ctx.xbf[:, slot, :]
    else:
        src = src_ap

    @block.tensor
    def _(t):
        wait(t, ctx.sVE)
        wait(t, ctx.sIN, ctx.const_ready_in)
        for j in range(8):
            i = t.transpose(out=ctx.p6[:, j * 128:(j + 1) * 128],
                            in_=src[:, j * 128:(j + 1) * 128], identity=ctx.idbf[:])
        ctx.sPE.inc(i)
    tdone = ctx.sPE.v

    @block.vector
    def _(v):
        wait(v, ctx.sPE, tdone)
        for sem, val in evac_extra_waits:
            wait(v, sem, val)
        ctx.sVE.inc(v.tensor_copy(
            out=dst_bf[:, 0:8, tile_idx * 128:tile_idx * 128 + 128],
            in_=ctx.p6[:].rearrange("p (j t) -> p j t", j=8)))


def emit_l2norm_proj(ctx, block, wsel, dst_ap, rhs_sel, sq_sb, raw_sb):
    @block.tensor
    def _(t):
        wait(t, ctx.sVE)
        for k in range(8):
            i = t.matmul(out=ctx.p0[:], lhsT=wsel(k), rhs=rhs_sel(k),
                         start=(k == 0), stop=(k == 7))
        ctx.sPE.inc(i)
    mm_done = ctx.sPE.v

    @block.vector
    def _(v):
        wait(v, ctx.sPE, mm_done)
        v.tensor_copy(out=raw_sb[:], in_=ctx.p0[:])
        ctx.sVE.inc(v.tensor_tensor(out=sq_sb[:], in0=raw_sb[:], in1=raw_sb[:],
                                    op=ALU.mult))
    evac_done = ctx.sVE.v

    @block.tensor
    def _(t):
        wait(t, ctx.sVE, evac_done)
        wait(t, ctx.sIN, ctx.const_ready_in)
        ctx.sPE.inc(t.matmul(out=ctx.p3[0:2, :], lhsT=ctx.hsel[:], rhs=sq_sb[:],
                             start=True, stop=True))
    ss_done = ctx.sPE.v

    @block.vector
    def _(v):
        wait(v, ctx.sPE, ss_done)
        ctx.sVE.inc(v.tensor_scalar(out=ctx.ss_sb[0:2, :], in0=ctx.p3[0:2, :],
                                    scalar1=6e-20, scalar2=None, op0=ALU.max))
    clip_done = ctx.sVE.v

    @block.scalar
    def _(s):
        wait(s, ctx.sVE, clip_done)
        s.activation(out=ctx.rn_sb[0:2, :], in_=ctx.ss_sb[0:2, :], func=AF.Ln)
        ctx.sSC.inc(s.activation(out=ctx.rn_sb[0:2, :], in_=ctx.rn_sb[0:2, :],
                                 func=AF.Exp, scale=-0.5))
    rn_done = ctx.sSC.v

    @block.tensor
    def _(t):
        wait(t, ctx.sSC, rn_done)
        wait(t, ctx.sVE)
        ctx.sPE.inc(t.matmul(out=ctx.p7[:], lhsT=ctx.hselt[:], rhs=ctx.rn_sb[0:2, :],
                             start=True, stop=True))
    bc_done = ctx.sPE.v

    @block.vector
    def _(v):
        wait(v, ctx.sPE, bc_done)
        ctx.sVE.inc(v.tensor_tensor(out=dst_ap, in0=raw_sb[:], in1=ctx.p7[:],
                                    op=ALU.mult))


def emit_ln(ctx, block, y_psum, x_dram, tile_idx, out_dram, slot):
    r = slot % ctx.nres
    zslot = slot % ctx.nz
    oslot = slot % ctx.nout

    @block.sync
    def _(sync):
        wait(sync, ctx.sVE)
        ctx.sIN.inc(sync.dma_start(
            out=ctx.xres[:, r, :],
            in_=x_dram[tile_idx * 128:(tile_idx + 1) * 128, :]), 16)
    res_done = ctx.sIN.v

    @block.vector
    def _(v):
        wait(v, ctx.sIN, res_done)
        wait(v, ctx.sPE)
        wait(v, ctx.sSC)
        v.tensor_tensor(out=ctx.z_sb[:, zslot, :], in0=y_psum,
                        in1=ctx.xres[:, r, :], op=ALU.add)
        z = ctx.z_sb[:, zslot, :]
        v.tensor_scalar(out=ctx.zsq[:], in0=z, scalar1=1.0 / DM, scalar2=0.0,
                        op0=ALU.mult, op1=ALU.add, accum_out=ctx.lnst[:, 0:1])
        ctx.sVE.inc(v.scalar_tensor_tensor(out=ctx.zsq[:], in0=z, scalar=1.0 / DM,
                                           in1=z, op0=ALU.mult, op1=ALU.mult,
                                           accum_out=ctx.lnst[:, 1:2]))
    z_done = ctx.sVE.v

    @block.gpsimd
    def _(g):
        wait(g, ctx.sVE, z_done)
        g.tensor_scalar(out=ctx.lnst[:, 2:3], in0=ctx.lnst[:, 0:1],
                        scalar1=-1.0, scalar2=None, op0=ALU.mult)
        ctx.sGP.inc(g.tensor_tensor(out=ctx.lnst[:, 3:4], in0=ctx.lnst[:, 0:1],
                                    in1=ctx.lnst[:, 0:1], op=ALU.mult))
    g_done = ctx.sGP.v

    @block.vector
    def _(v):
        wait(v, ctx.sGP, g_done)
        ctx.sVE.inc(v.tensor_tensor(out=ctx.lnst[:, 4:5], in0=ctx.lnst[:, 1:2],
                                    in1=ctx.lnst[:, 3:4], op=ALU.subtract))
    var_done = ctx.sVE.v

    @block.scalar
    def _(s):
        wait(s, ctx.sVE, var_done)
        wait(s, ctx.sST)
        s.activation(out=ctx.lnst[:, 6:7], in_=ctx.lnst[:, 4:5], func=AF.Ln,
                     bias=ctx.ceps[:])
        s.activation(out=ctx.lnst[:, 6:7], in_=ctx.lnst[:, 6:7],
                     func=AF.Exp, scale=-0.5)
        s.activation(out=ctx.x1o[:, oslot, :], in_=ctx.z_sb[:, zslot, :],
                     func=AF.Identity, bias=ctx.lnst[:, 2:3])
        ctx.sSC.inc(s.activation(out=ctx.x1o[:, oslot, :], in_=ctx.x1o[:, oslot, :],
                                 func=AF.Identity, scale=ctx.lnst[:, 6:7]))
    app_done = ctx.sSC.v

    @block.scalar
    def _(s):
        wait(s, ctx.sSC, app_done)
        ctx.sST.inc(s.dma_start(
            out=out_dram[tile_idx * 128:(tile_idx + 1) * 128, :],
            in_=ctx.x1o[:, oslot, :]), 16)


def emit_final_wait(ctx, block):
    @block.scalar
    def _(s):
        wait(s, ctx.sST)


def build_p0(nc, x0_d, wq_d, wkv_d, wo_d, x1_d, dbg=None, pfx="", cdram=None):
    with contextlib.ExitStack() as st:
        ctx = Ctx()
        common_setup(ctx, st, nc, slim=True, pfx=pfx, cdram=cdram)
        wq_bf = st.enter_context(nc.sbuf_tensor(pfx + "wq_bf", [128, 8, 512], BF16))
        wkv_bf = st.enter_context(nc.sbuf_tensor(pfx + "wkv_bf", [128, 8, 1024], BF16))
        wo_bf = st.enter_context(nc.sbuf_tensor(pfx + "wo_bf", [128, 4, 1024], BF16))
        x0t = st.enter_context(nc.sbuf_tensor(pfx + "x0t", [128, 8, 512], BF16))
        kt_ = st.enter_context(nc.sbuf_tensor(pfx + "kt", [128, 4, SEQ], BF16))
        qt = st.enter_context(nc.sbuf_tensor(pfx + "qt", [128, 4, SEQ], BF16))
        v_sb = st.enter_context(nc.sbuf_tensor(pfx + "v_sb", [128, NT, 520], BF16))
        ctxT = st.enter_context(nc.sbuf_tensor(pfx + "ctxT", [128, 4, 512], BF16))
        ptP = st.enter_context(nc.sbuf_tensor(pfx + "ptP", [128, 2, 1024], BF16))
        ptA = ptP[:, :, 0:512]
        ptB = ptP[:, :, 512:1024]
        raw = st.enter_context(nc.sbuf_tensor(pfx + "raw", [128, 2, 512], F32))
        sq_sb = st.enter_context(nc.sbuf_tensor(pfx + "sq_sb", [128, 512], F32))
        ctx.ss_sb = st.enter_context(nc.sbuf_tensor(pfx + "ss_sb", [2, 512], F32))
        ctx.rn_sb = st.enter_context(nc.sbuf_tensor(pfx + "rn_sb", [2, 512], F32))
        scA = st.enter_context(nc.sbuf_tensor(pfx + "scA", [1, 512], F32))
        scB = st.enter_context(nc.sbuf_tensor(pfx + "scB", [1, 512], F32))
        rsA = st.enter_context(nc.sbuf_tensor(pfx + "rsA", [1, 512], F32))
        rsB = st.enter_context(nc.sbuf_tensor(pfx + "rsB", [1, 512], F32))
        bcvA = st.enter_context(nc.sbuf_tensor(pfx + "bcvA", [64, 512], F32))
        bcvB = st.enter_context(nc.sbuf_tensor(pfx + "bcvB", [64, 512], F32))

        block = st.enter_context(nc.Block())
        emit_const_loads(ctx, block)
        emit_weight_load(ctx, block, wq_bf, wq_d, 512, 8)
        emit_weight_load(ctx, block, wkv_bf, wkv_d, 1024, 8, ring_base=8)
        emit_weight_load(ctx, block, wo_bf, wo_d, 1024, 4, ring_base=16)

        @block.vector
        def _(v):
            for h in range(8):
                i = v.memset(v_sb[:, :, h * 65 + 64:h * 65 + 65], 1.0)
            ctx.sVE.inc(i)

        for c in range(NCH):
            for tt in range(4):
                g = c * 4 + tt
                @block.sync
                def _(sync):
                    wait(sync, ctx.sVE)
                    ctx.sIN.inc(sync.dma_start(
                        out=ctx.xin[:, g % ctx.nring, :],
                        in_=x0_d[g * 128:(g + 1) * 128, :]), 16)
                ld = ctx.sIN.v

                @block.vector
                def _(v):
                    wait(v, ctx.sIN, ld)
                emit_tokmaj_to_dmajor(ctx, block, ctx.xin[:, g % ctx.nring, :], x0t, tt)

            chains = []
            for wmat, dst in ((wkv_bf, kt_), (wq_bf, qt)):
                for j in range(4):
                    chains.append((wmat, j, dst[:, j, c * 512:(c + 1) * 512]))
            mm_te = {}
            sq_ve = {}
            ss_te = {}
            clip_ve = {}
            rn_sc = {}
            bc_te = {}
            apply_ve = {}

            def emit_bc_apply(i):
                @block.tensor
                def _(t):
                    wait(t, ctx.sSC, rn_sc[i])
                    if i >= 1:
                        wait(t, ctx.sVE, apply_ve[i - 1])
                    ctx.sPE.inc(t.matmul(out=ctx.p7[:], lhsT=ctx.hselt[:],
                                         rhs=ctx.rn_sb[0:2, :], start=True,
                                         stop=True, skip_group_check=True))
                bc_te[i] = ctx.sPE.v

                @block.vector
                def _(v):
                    wait(v, ctx.sPE, bc_te[i])
                    ctx.sVE.inc(v.tensor_tensor(out=chains[i][2],
                                                in0=raw[:, i % 2, :], in1=ctx.p7[:],
                                                op=ALU.mult))
                apply_ve[i] = ctx.sVE.v

            for i, (wmat, j, dst_ap) in enumerate(chains):
                @block.tensor
                def _(t):
                    if i >= 1:
                        wait(t, ctx.sVE, sq_ve[i - 1])
                    else:
                        wait(t, ctx.sVE)
                    for k in range(8):
                        mmi = t.matmul(out=ctx.p0[:], lhsT=wmat[:, k, j * 128:(j + 1) * 128],
                                       rhs=x0t[:, k, :], start=(k == 0), stop=(k == 7),
                                       skip_group_check=True)
                    ctx.sPE.inc(mmi)
                mm_te[i] = ctx.sPE.v
                if i >= 1:
                    emit_bc_apply(i - 1)

                @block.vector
                def _(v):
                    wait(v, ctx.sPE, mm_te[i])
                    if i >= 1:
                        wait(v, ctx.sPE, ss_te[i - 1])
                    v.tensor_copy(out=raw[:, i % 2, :], in_=ctx.p0[:])
                    ctx.sVE.inc(v.tensor_tensor(out=sq_sb[:], in0=raw[:, i % 2, :],
                                                in1=raw[:, i % 2, :], op=ALU.mult))
                sq_ve[i] = ctx.sVE.v

                @block.tensor
                def _(t):
                    wait(t, ctx.sVE, sq_ve[i])
                    wait(t, ctx.sIN, ctx.const_ready_in)
                    if i >= 1:
                        wait(t, ctx.sVE, clip_ve[i - 1])
                    ctx.sPE.inc(t.matmul(out=ctx.p3[0:2, :], lhsT=ctx.hsel[:],
                                         rhs=sq_sb[:], start=True, stop=True,
                                         skip_group_check=True))
                ss_te[i] = ctx.sPE.v

                @block.vector
                def _(v):
                    wait(v, ctx.sPE, ss_te[i])
                    if i >= 1:
                        wait(v, ctx.sSC, rn_sc[i - 1])
                    ctx.sVE.inc(v.tensor_scalar(out=ctx.ss_sb[0:2, :], in0=ctx.p3[0:2, :],
                                                scalar1=6e-20, scalar2=None, op0=ALU.max))
                clip_ve[i] = ctx.sVE.v

                @block.scalar
                def _(s):
                    wait(s, ctx.sVE, clip_ve[i])
                    if i >= 1:
                        wait(s, ctx.sPE, bc_te[i - 1])
                    s.activation(out=ctx.rn_sb[0:2, :], in_=ctx.ss_sb[0:2, :], func=AF.Ln)
                    ctx.sSC.inc(s.activation(out=ctx.rn_sb[0:2, :], in_=ctx.rn_sb[0:2, :],
                                             func=AF.Exp, scale=-0.5))
                rn_sc[i] = ctx.sSC.v
            emit_bc_apply(7)

            for tt in range(4):
                g = c * 4 + tt
                @block.tensor
                def _(t):
                    wait(t, ctx.sVE)
                    for k in range(8):
                        i = t.matmul(out=ctx.p45[:, 0:512],
                                     lhsT=x0t[:, k, tt * 128:(tt + 1) * 128],
                                     rhs=wkv_bf[:, k, 512:1024],
                                     start=(k == 0), stop=(k == 7))
                    ctx.sPE.inc(i)
                vm = ctx.sPE.v

                @block.vector
                def _(v):
                    wait(v, ctx.sPE, vm)
                    ctx.sVE.inc(v.tensor_copy(
                        out=v_sb[:, g, 0:520].rearrange("p (h d) -> p h d", d=65)[:, :, 0:64],
                        in_=ctx.p45[:, 0:512].rearrange("p (h d) -> p h d", d=64)))

        prev_exp_last = 0
        prev_bcv = 0
        lnz_guard = 0
        prev_bcv_ve = 0
        prev_exp_even = 0
        for c in range(NCH):
            for jo in range(4):
                hA, hB = 2 * jo, 2 * jo + 1
                cnt = ctx.sSC.v
                expA = []
                expB = []
                for _kt in range(NT):
                    if _kt % 2 == 0:
                        expA.append(cnt + 1)
                        expB.append(cnt + 2)
                        cnt += 2
                    else:
                        cnt += 1
                        expA.append(cnt)
                        expB.append(cnt)
                rsA_proj = cnt + 1
                rsB_proj = cnt + 2
                sVE_all = ctx.sVE.v
                qk_act = []
                pv_act = []

                @block.tensor
                def _(t):
                    def emit_pv(pv, waited):
                        if pv == 0:
                            wait(t, ctx.sVE, sVE_all)
                        if not waited:
                            wait_f(t, ctx.sSC, expB[pv])
                        t.matmul(out=ctx.p2[0:65, :],
                                 lhsT=v_sb[:, pv, hA * 65:hA * 65 + 65],
                                 rhs=ptA[:, pv % 2, :],
                                 start=(pv == 0), stop=(pv == NT - 1),
                                 skip_group_check=True)
                        ctx.sPE.inc(t.matmul(
                            out=ctx.p3[0:65, :],
                            lhsT=v_sb[:, pv, hB * 65:hB * 65 + 65],
                            rhs=ptB[:, pv % 2, :],
                            start=(pv == 0), stop=(pv == NT - 1),
                            skip_group_check=True))
                        pv_act.append(ctx.sPE.v)
                    for kt in range(NT):
                        if kt == 0:
                            if prev_exp_even:
                                wait(t, ctx.sSC, prev_exp_even)
                            if c == 0 and jo == 0:
                                wait(t, ctx.sSC)
                                wait(t, ctx.sVE, sVE_all)
                        if kt == 1:
                            if prev_exp_last:
                                wait(t, ctx.sSC, prev_exp_last)
                            if jo == 0:
                                wait(t, ctx.sVE, lnz_guard)
                            if prev_bcv_ve:
                                wait(t, ctx.sVE, prev_bcv_ve)
                        if kt >= 2:
                            wait_f(t, ctx.sSC, expB[kt - 2])
                        if kt % 2 == 0:
                            plA, plB = ctx.p0[:], ctx.p1[:]
                        else:
                            plA, plB = ctx.p45[:, 0:512], ctx.p45[:, 512:1024]
                        t.matmul(out=plA,
                                 lhsT=kt_[0:64, jo, kt * 128:(kt + 1) * 128],
                                 rhs=qt[0:64, jo, c * 512:(c + 1) * 512],
                                 start=True, stop=True, skip_group_check=True)
                        ctx.sPE.inc(t.matmul(
                            out=plB,
                            lhsT=kt_[64:128, jo, kt * 128:(kt + 1) * 128],
                            rhs=qt[64:128, jo, c * 512:(c + 1) * 512],
                            start=True, stop=True, skip_group_check=True))
                        qk_act.append(ctx.sPE.v)
                        if kt >= 2:
                            emit_pv(kt - 2, waited=True)
                    emit_pv(NT - 2, waited=False)
                    emit_pv(NT - 1, waited=False)
                    if prev_bcv_ve:
                        wait(t, ctx.sVE, prev_bcv_ve)
                    wait_f(t, ctx.sSC, rsA_proj)
                    t.matmul(out=ctx.p7[0:64, :], lhsT=ctx.ones64[:],
                             rhs=rsA[:], start=True, stop=True,
                             skip_group_check=True)
                    wait_f(t, ctx.sSC, rsB_proj)
                    ctx.sPE.inc(t.matmul(out=ctx.p45[0:64, 0:512], lhsT=ctx.ones64[:],
                                         rhs=rsB[:], start=True, stop=True,
                                         skip_group_check=True))
                bc_act = ctx.sPE.v

                @block.vector
                def _(v):
                    wait(v, ctx.sPE, pv_act[NT - 1])
                    v.tensor_scalar(out=scA[:], in0=ctx.p2[64:65, :],
                                    scalar1=1e-6, scalar2=None, op0=ALU.max)
                    ctx.sVE.inc(v.tensor_scalar(out=scB[:], in0=ctx.p3[64:65, :],
                                                scalar1=1e-6, scalar2=None, op0=ALU.max))
                    clip_act = ctx.sVE.v
                    wait(v, ctx.sPE, bc_act)
                    v.tensor_copy(out=bcvA[:], in_=ctx.p7[0:64, :])
                    v.tensor_copy(out=bcvB[:], in_=ctx.p45[0:64, 0:512])
                    v.tensor_tensor(out=ctxT[0:64, jo, :],
                                    in0=ctx.p2[0:64, :], in1=bcvA[:],
                                    op=ALU.mult)
                    ctx.sVE.inc(v.tensor_tensor(out=ctxT[64:128, jo, :],
                                                in0=ctx.p3[0:64, :], in1=bcvB[:],
                                                op=ALU.mult))
                clip_act = ctx.sVE.v - 1
                prev_bcv_ve = ctx.sVE.v

                @block.scalar
                def _(s):
                    for kt in range(NT):
                        wait(s, ctx.sPE, qk_act[kt])
                        if kt >= 2:
                            wait(s, ctx.sPE, pv_act[kt - 2])
                        if kt % 2 == 0:
                            plA, plB = ctx.p0[:], ctx.p1[:]
                        else:
                            plA, plB = ctx.p45[:, 0:512], ctx.p45[:, 512:1024]
                        if kt % 2 == 1:
                            ctx.sSC.inc(s.activation(
                                out=ptP[:, kt % 2, :], in_=ctx.p45[:, 0:1024],
                                func=AF.Exp, scale=16.0, bias=ctx.cneg16[:]))
                        else:
                            ctx.sSC.inc(s.activation(out=ptA[:, kt % 2, :], in_=plA,
                                                     func=AF.Exp, scale=16.0,
                                                     bias=ctx.cneg16[:]))
                            ctx.sSC.inc(s.activation(out=ptB[:, kt % 2, :], in_=plB,
                                                     func=AF.Exp, scale=16.0,
                                                     bias=ctx.cneg16[:]))
                    wait(s, ctx.sVE, clip_act)
                    s.activation(out=rsA[:], in_=scA[:], func=AF.Ln)
                    ctx.sSC.inc(s.activation(out=rsA[:], in_=rsA[:],
                                             func=AF.Exp, scale=-1.0))
                    s.activation(out=rsB[:], in_=scB[:], func=AF.Ln)
                    ctx.sSC.inc(s.activation(out=rsB[:], in_=rsB[:],
                                             func=AF.Exp, scale=-1.0))
                assert ctx.sSC.v == rsB_proj, (ctx.sSC.v, rsB_proj)
                prev_exp_last = expB[NT - 1]
                prev_exp_even = expB[NT - 2]

            for tt in range(4):
                g = c * 4 + tt
                @block.tensor
                def _(t):
                    wait(t, ctx.sVE)
                    for half in range(2):
                        for k in range(4):
                            i = t.matmul(out=ctx.p45[:, half * 512:(half + 1) * 512],
                                         lhsT=ctxT[:, k, tt * 128:(tt + 1) * 128],
                                         rhs=wo_bf[:, k, half * 512:(half + 1) * 512],
                                         start=(k == 0), stop=(k == 3))
                    ctx.sPE.inc(i)
                emit_ln(ctx, block, ctx.p45[:], x0_d, g, x1_d, g)
            lnz_guard = ctx.sVE.v

        emit_final_wait(ctx, block)
    return nc


def build_p1(nc, x1_d, bw_d, bb_d, la_d, lb_d, rw_d, rb_d, x2_d, dbg=None, pfx="", cdram=None):
    with contextlib.ExitStack() as st:
        ctx = Ctx()
        common_setup(ctx, st, nc, pfx=pfx, cdram=cdram)
        bwT_bf = st.enter_context(nc.sbuf_tensor(pfx + "bwT_bf", [128, 8, 1024], BF16))
        la_bf = st.enter_context(nc.sbuf_tensor(pfx + "la_bf", [128, 8, 8], BF16))
        rwT_bf = st.enter_context(nc.sbuf_tensor(pfx + "rwT_bf", [128, 8, 8], BF16))
        b2_bf = st.enter_context(nc.sbuf_tensor(pfx + "b2_bf", [64, 1024], BF16))
        bb_sb = st.enter_context(nc.sbuf_tensor(pfx + "bb_sb", [1, 1024], F32))
        rb_sb = st.enter_context(nc.sbuf_tensor(pfx + "rb_sb", [1, 8], F32))
        rw_sb = st.enter_context(nc.sbuf_tensor(pfx + "rw_sb", [8, 1024], F32))
        rwbf_sb = st.enter_context(nc.sbuf_tensor(pfx + "rwbf_sb", [8, 1024], BF16))
        b2f_sb = st.enter_context(nc.sbuf_tensor(pfx + "b2f_sb", [64, 1024], F32))
        x1t = st.enter_context(nc.sbuf_tensor(pfx + "x1t", [128, 8, 128], BF16))
        eg_sb = st.enter_context(nc.sbuf_tensor(pfx + "eg_sb", [128, 8], F32))
        g_bf = st.enter_context(nc.sbuf_tensor(pfx + "g_bf", [128, 8], BF16))
        g_st = st.enter_context(nc.sbuf_tensor(pfx + "g_st", [128, 4], F32))
        c_bf = st.enter_context(nc.sbuf_tensor(pfx + "c_bf", [128, 64], BF16))
        ct_sb = st.enter_context(nc.sbuf_tensor(pfx + "ct_sb", [64, 128], BF16))

        block = st.enter_context(nc.Block())
        emit_const_loads(ctx, block)
        for r in range(8):
            @block.sync
            def _(sync):
                wait(sync, ctx.sVE)
                ctx.sIN.inc(sync.dma_start(out=ctx.xin[:, r % ctx.nring, :],
                                           in_=bw_d[r * 128:(r + 1) * 128, :]), 16)
            ld = ctx.sIN.v

            @block.vector
            def _(v):
                wait(v, ctx.sIN, ld)
            emit_tokmaj_to_dmajor(ctx, block, ctx.xin[:, r % ctx.nring, :], bwT_bf, r)

        @block.sync
        def _(sync):
            ctx.sIN.inc(sync.dma_start(out=bb_sb[:], in_=bb_d.unsqueeze(0)), 16)
            ctx.sIN.inc(sync.dma_start(out=rb_sb[:], in_=rb_d.unsqueeze(0)), 16)
            ctx.sIN.inc(sync.dma_start(out=rw_sb[:], in_=rw_d), 16)
            ctx.sIN.inc(sync.dma_start(out=b2f_sb[:], in_=lb_d.rearrange("e r d -> (e r) d")), 16)
            ctx.sIN.inc(sync.dma_start(out=ctx.xin[:, 0, 0:8].rearrange("p a -> p a"),
                                       in_=la_d.rearrange("(k p) r -> p k r", p=128)
                                       if False else la_d[0:128, :]), 16)
        @block.sync
        def _(sync):
            wait(sync, ctx.sVE)
            ctx.sIN.inc(sync.dma_start(
                out=ctx.xin[:, 1, 0:64].rearrange("p (k r) -> p k r", k=8),
                in_=la_d.rearrange("(k p) r -> p k r", p=128)), 16)
        la_ld = ctx.sIN.v
        ctx.smallw_ld = la_ld

        @block.vector
        def _(v):
            wait(v, ctx.sIN, la_ld)
            v.tensor_copy(out=la_bf[:], in_=ctx.xin[:, 1, 0:64].rearrange("p (k r) -> p k r", k=8))
            v.tensor_copy(out=b2_bf[:], in_=b2f_sb[:])
            ctx.sVE.inc(v.tensor_copy(out=rwbf_sb[:], in_=rw_sb[:]))
        smallw_done = ctx.sVE.v

        for j in range(8):
            @block.tensor
            def _(t):
                wait(t, ctx.sVE, smallw_done)
                wait(t, ctx.sVE)
                wait(t, ctx.sIN, ctx.const_ready_in)
                ctx.sPE.inc(t.transpose(out=ctx.p6[:, 0:8], in_=rwbf_sb[0:8, j * 128:(j + 1) * 128],
                                        identity=ctx.idbf[0:8, 0:8]))
            td = ctx.sPE.v

            @block.vector
            def _(v):
                wait(v, ctx.sPE, td)
                ctx.sVE.inc(v.tensor_copy(out=rwT_bf[:, j, :], in_=ctx.p6[:, 0:8]))

        for g in range(NT):
            @block.sync
            def _(sync):
                wait(sync, ctx.sVE)
                ctx.sIN.inc(sync.dma_start(out=ctx.xin[:, g % ctx.nring, :],
                                           in_=x1_d[g * 128:(g + 1) * 128, :]), 16)
            ld = ctx.sIN.v

            @block.vector
            def _(v):
                wait(v, ctx.sIN, ld)
            emit_tokmaj_to_dmajor(ctx, block, ctx.xin[:, g % ctx.nring, :], x1t, 0)
            xt_done = ctx.sVE.v

            @block.tensor
            def _(t):
                wait(t, ctx.sVE, xt_done)
                wait(t, ctx.sIN, ctx.smallw_ld)
                wait(t, ctx.sSC)
                wait(t, ctx.sVE)
                for k in range(8):
                    t.matmul(out=ctx.p0[:, 0:8], lhsT=x1t[:, k, :], rhs=la_bf[:, k, :],
                             start=(k == 0), stop=(k == 7))
                for k in range(8):
                    i = t.matmul(out=ctx.p1[:, 0:8], lhsT=x1t[:, k, :], rhs=rwT_bf[:, k, :],
                                 start=(k == 0), stop=False, skip_group_check=True)
                ctx.sPE.inc(t.matmul(out=ctx.p1[:, 0:8], lhsT=ctx.ones128[:], rhs=rb_sb[:],
                                     start=False, stop=True, skip_group_check=True))
                for half in range(2):
                    for k in range(8):
                        t.matmul(out=ctx.p45[:, half * 512:(half + 1) * 512],
                                 lhsT=x1t[:, k, :],
                                 rhs=bwT_bf[:, k, half * 512:(half + 1) * 512],
                                 start=(k == 0), stop=False, skip_group_check=True)
                    i = t.matmul(out=ctx.p45[:, half * 512:(half + 1) * 512],
                                 lhsT=ctx.ones128[:], rhs=bb_sb[:, half * 512:(half + 1) * 512],
                                 start=False, stop=False, skip_group_check=True)
                ctx.sPE.inc(i)
            midrt_done = ctx.sPE.v - 1

            @block.vector
            def _(v):
                wait(v, ctx.sPE, midrt_done)
                ctx.sVE.inc(v.reduce_max(out=g_st[:, 0:1], in_=ctx.p1[:, 0:8], axis=AX.X))
            mx_done = ctx.sVE.v

            @block.gpsimd
            def _(gp):
                wait(gp, ctx.sVE, mx_done)
                ctx.sGP.inc(gp.tensor_scalar(out=g_st[:, 1:2], in0=g_st[:, 0:1],
                                             scalar1=-1.0, scalar2=None, op0=ALU.mult))
            nmx_done = ctx.sGP.v

            @block.scalar
            def _(s):
                wait(s, ctx.sGP, nmx_done)
                ctx.sSC.inc(s.activation(out=eg_sb[:], in_=ctx.p1[:, 0:8], func=AF.Exp,
                                         bias=g_st[:, 1:2]))
            eg_done = ctx.sSC.v

            @block.vector
            def _(v):
                wait(v, ctx.sSC, eg_done)
                ctx.sVE.inc(v.reduce_sum(out=g_st[:, 2:3], in_=eg_sb[:], axis=AX.X))
            sm_done = ctx.sVE.v

            @block.scalar
            def _(s):
                wait(s, ctx.sVE, sm_done)
                s.activation(out=g_st[:, 3:4], in_=g_st[:, 2:3], func=AF.Ln)
                s.activation(out=g_st[:, 3:4], in_=g_st[:, 3:4], func=AF.Exp, scale=-1.0)
                ctx.sSC.inc(s.activation(out=g_bf[:], in_=eg_sb[:], func=AF.Identity,
                                         scale=g_st[:, 3:4]))
            gbf_done = ctx.sSC.v

            @block.vector
            def _(v):
                wait(v, ctx.sSC, gbf_done)
                ctx.sVE.inc(v.scalar_tensor_tensor(
                    out=c_bf[:].rearrange("p (e r) -> p e r", e=8),
                    in0=ctx.p0[:, 0:8].unsqueeze(1).broadcast_to([128, 8, 8]),
                    scalar=2.0,
                    in1=g_bf[:].unsqueeze(2).broadcast_to([128, 8, 8]),
                    op0=ALU.mult, op1=ALU.mult))
            c_done = ctx.sVE.v

            @block.tensor
            def _(t):
                wait(t, ctx.sVE, c_done)
                ctx.sPE.inc(t.transpose(out=ctx.p6[0:64, 0:128], in_=c_bf[:],
                                        identity=ctx.idbf[:]))
            ct_done = ctx.sPE.v

            @block.vector
            def _(v):
                wait(v, ctx.sPE, ct_done)
                ctx.sVE.inc(v.tensor_copy(out=ct_sb[:], in_=ctx.p6[0:64, 0:128]))
            cte_done = ctx.sVE.v

            @block.tensor
            def _(t):
                wait(t, ctx.sVE, cte_done)
                for half in range(2):
                    ctx.sPE.inc(t.matmul(out=ctx.p45[:, half * 512:(half + 1) * 512],
                                         lhsT=ct_sb[:], rhs=b2_bf[:, half * 512:(half + 1) * 512],
                                         start=False, stop=True, skip_group_check=True))
            emit_ln(ctx, block, ctx.p45[:], x1_d, g, x2_d, g)

        emit_final_wait(ctx, block)
    return nc


def build_p2(nc, x2_d, wqkv_d, wo_d, x3_d, qspill, kspill, pfx="", cdram=None):
    NH = 16
    with contextlib.ExitStack() as st:
        ctx = Ctx()
        common_setup(ctx, st, nc, want=("IDBF", "TRIT"),
                     slim=True, pfx=pfx, cdram=cdram)
        if cdram is not None:
            ropec_d = cdram["ROPE_COS"]
            ropes_d = cdram["ROPE_SIN"]
        else:
            ropec_d = nc.dram_tensor("ROPE_COS", [128, NT, 32], F32, kind="ExternalInput").ap()
            ropes_d = nc.dram_tensor("ROPE_SIN", [128, NT, 32], F32, kind="ExternalInput").ap()
        ropb = st.enter_context(nc.sbuf_tensor(pfx + "ropb", [128, 2, 2, 32], F32))
        big8 = st.enter_context(nc.sbuf_tensor(pfx + "big8", [128, 8, SEQ], BF16))
        wqkv_bf = big8[:, :, 0:3072].rearrange("p k n -> p k n")
        kt_sb = big8
        wo_bf = st.enter_context(nc.sbuf_tensor(pfx + "wo_bf", [128, 8, 1024], BF16))
        v_sb = st.enter_context(nc.sbuf_tensor(pfx + "v_sb", [128, NT, NH * 65], BF16))
        x2t_t = st.enter_context(nc.sbuf_tensor(pfx + "x2t_t", [128, 8, 128], BF16))
        qk_sp = st.enter_context(nc.sbuf_tensor(pfx + "qk_sp", [128, 8, 128], BF16))
        ctxT = st.enter_context(nc.sbuf_tensor(pfx + "ctxT2", [128, 8, 512], BF16))
        ptA = st.enter_context(nc.sbuf_tensor(pfx + "pt2A", [128, 2, 512], BF16))
        ptB = st.enter_context(nc.sbuf_tensor(pfx + "pt2B", [128, 2, 512], BF16))
        qss = st.enter_context(nc.sbuf_tensor(pfx + "qss", [128, NT, NH], BF16))
        kss = st.enter_context(nc.sbuf_tensor(pfx + "kss", [128, NT, NH], BF16))
        mrow = st.enter_context(nc.sbuf_tensor(pfx + "mrow", [1, 32], F32))
        lrow = st.enter_context(nc.sbuf_tensor(pfx + "lrow", [1, 32], F32))
        urow = st.enter_context(nc.sbuf_tensor(pfx + "urow", [1, 16], F32))
        ub_sb = st.enter_context(nc.sbuf_tensor(pfx + "ub_sb", [128, 16], F32))
        mm2 = st.enter_context(nc.sbuf_tensor(pfx + "mm2", [32, 1], BF16))
        stat2 = st.enter_context(nc.sbuf_tensor(pfx + "stat2", [128, 32], BF16))
        scp = st.enter_context(nc.sbuf_tensor(pfx + "scp2", [33, 512], F32))
        scA = scp[0:1, :]
        scB = scp[32:33, :]
        rsA = st.enter_context(nc.sbuf_tensor(pfx + "rsA2", [1, 512], F32))
        rsB = st.enter_context(nc.sbuf_tensor(pfx + "rsB2", [1, 512], F32))
        bcvA = st.enter_context(nc.sbuf_tensor(pfx + "bcvA2", [64, 512], F32))
        bcvB = st.enter_context(nc.sbuf_tensor(pfx + "bcvB2", [64, 512], F32))

        qtc = ctx.xin[:].rearrange("p a b -> p (a b)").bitcast(BF16).rearrange(
            "p (j t) -> p j t", j=8)
        block = st.enter_context(nc.Block())
        emit_const_loads(ctx, block)
        for k in range(8):
            for piece in range(3):
                @block.sync
                def _(sync):
                    wait(sync, ctx.sVE)
                    ctx.sIN.inc(sync.dma_start(
                        out=ctx.xin[:, (k * 3 + piece) % ctx.nring, :],
                        in_=wqkv_d[k * 128:(k + 1) * 128, piece * 1024:(piece + 1) * 1024]), 16)
                ld = ctx.sIN.v

                @block.vector
                def _(v):
                    wait(v, ctx.sIN, ld)
                    ctx.sVE.inc(v.tensor_copy(
                        out=big8[:, k, piece * 1024:(piece + 1) * 1024],
                        in_=ctx.xin[:, (k * 3 + piece) % ctx.nring, :]))
        emit_weight_load(ctx, block, wo_bf, wo_d, 1024, 8, ring_base=0)

        @block.vector
        def _(v):
            for h in range(NH):
                i = v.memset(v_sb[:, :, h * 65 + 64:h * 65 + 65], 1.0)
            ctx.sVE.inc(i)

        for g in range(NT):
            @block.sync
            def _(sync):
                wait(sync, ctx.sVE)
                ctx.sIN.inc(sync.dma_start(out=ctx.xin[:, g % ctx.nring, :],
                                           in_=x2_d[g * 128:(g + 1) * 128, :]), 16)
                wait(sync, ctx.sGP)
                ctx.sIN.inc(sync.dma_start(out=ropb[:, g % 2, 0, :], in_=ropec_d[:, g, :]), 16)
                ctx.sIN.inc(sync.dma_start(out=ropb[:, g % 2, 1, :], in_=ropes_d[:, g, :]), 16)
            ld = ctx.sIN.v

            @block.vector
            def _(v):
                wait(v, ctx.sIN, ld)
            emit_tokmaj_to_dmajor(ctx, block, ctx.xin[:, g % ctx.nring, :], x2t_t, 0)
            xt_done = ctx.sVE.v

            for comp, spill, ss_all in (("q", qspill, qss), ("k", kspill, kss)):
                col0 = 0 if comp == "q" else 1024
                @block.tensor
                def _(t):
                    wait(t, ctx.sVE)
                    wait(t, ctx.sGP)
                    for half in range(2):
                        for k in range(8):
                            i = t.matmul(out=ctx.p45[:, half * 512:(half + 1) * 512],
                                         lhsT=x2t_t[:, k, :],
                                         rhs=wqkv_bf[:, k, col0 + half * 512:col0 + (half + 1) * 512],
                                         start=(k == 0), stop=(k == 7), skip_group_check=True)
                    ctx.sPE.inc(i)
                proj_done = ctx.sPE.v

                q3 = ctx.p45[:].rearrange("p (h two c) -> p h two c", h=NH, two=2)
                cosb = ropb[:, g % 2, 0, :].unsqueeze(1).broadcast_to([128, NH, 32])
                sinb = ropb[:, g % 2, 1, :].unsqueeze(1).broadcast_to([128, NH, 32])
                ra = ctx.z_sb[:, 0, :].rearrange("p (h two c) -> p h two c", h=NH, two=2)
                rb = ctx.z_sb[:, 1, :].rearrange("p (h two c) -> p h two c", h=NH, two=2)
                rope_bf = ctx.xbf[:, 1, :].rearrange("p (h two c) -> p h two c", h=NH, two=2)

                @block.vector
                def _(v):
                    wait(v, ctx.sPE, proj_done)
                    wait(v, ctx.sIN, ld)
                    wait(v, ctx.sGP)
                    v.tensor_tensor(out=ra[:, :, 0, :], in0=q3[:, :, 0, :], in1=cosb, op=ALU.mult)
                    v.scalar_tensor_tensor(out=rb[:, :, 0, :], in0=q3[:, :, 1, :], scalar=-1.0,
                                           in1=sinb, op0=ALU.mult, op1=ALU.mult)
                    v.tensor_tensor(out=ra[:, :, 1, :], in0=q3[:, :, 1, :], in1=cosb, op=ALU.mult)
                    ctx.sVE.inc(v.tensor_tensor(out=rb[:, :, 1, :], in0=q3[:, :, 0, :], in1=sinb,
                                                op=ALU.mult))
                rp_done = ctx.sVE.v

                @block.gpsimd
                def _(gp):
                    wait(gp, ctx.sVE, rp_done)
                    wait(gp, ctx.sSC)
                    wait(gp, ctx.sPE)
                    ctx.sGP.inc(gp.tensor_tensor(out=ctx.xbf[:, 1, :], in0=ctx.z_sb[:, 0, :],
                                                 in1=ctx.z_sb[:, 1, :], op=ALU.add))
                rope_done = ctx.sGP.v

                @block.vector
                def _(v):
                    wait(v, ctx.sGP, rope_done)
                    v.tensor_tensor(out=ctx.zsq[:], in0=ctx.xbf[:, 1, :],
                                    in1=ctx.xbf[:, 1, :], op=ALU.mult)
                    with nc.allow_low_precision(reason="softmax bound only; cancels"):
                        ctx.sVE.inc(v.reduce_sum(
                            out=ss_all[:, g, :],
                            in_=ctx.zsq[:].rearrange("p (h c) -> p h c", h=NH), axis=AX.X))

                emit_tokmaj_to_dmajor(ctx, block, ctx.xbf[:, 1, :], qk_sp, 0, src_is_bf=True,
                                      evac_extra_waits=[(ctx.sST, ctx.sST.v)])
                tr_done = ctx.sVE.v

                @block.scalar
                def _(s):
                    wait(s, ctx.sVE, tr_done)
                    ctx.sST.inc(s.dma_start(out=spill[:, :, g * 128:(g + 1) * 128],
                                            in_=qk_sp[:]), 16)

            @block.tensor
            def _(t):
                wait(t, ctx.sVE)
                wait(t, ctx.sGP)
                for half in range(2):
                    for k in range(8):
                        i = t.matmul(out=ctx.p45[:, half * 512:(half + 1) * 512],
                                     lhsT=x2t_t[:, k, :],
                                     rhs=wqkv_bf[:, k, 2048 + half * 512:2048 + (half + 1) * 512],
                                     start=(k == 0), stop=(k == 7), skip_group_check=True)
                ctx.sPE.inc(i)
            vm = ctx.sPE.v

            @block.vector
            def _(v):
                wait(v, ctx.sPE, vm)
                ctx.sVE.inc(v.tensor_copy(
                    out=v_sb[:, g, :].rearrange("p (h d) -> p h d", d=65)[:, :, 0:64],
                    in_=ctx.p45[:].rearrange("p (h d) -> p h d", d=64)))

        @block.vector
        def _(v):
            wait(v, ctx.sSC)
            wait(v, ctx.sVE)
            with nc.allow_low_precision(reason="softmax bound only; cancels"):
                v.reduce_max(out=stat2[:, 0:16], in_=qss[:].rearrange("p t h -> p h t"), axis=AX.X)
                ctx.sVE.inc(v.reduce_max(out=stat2[:, 16:32],
                                         in_=kss[:].rearrange("p t h -> p h t"), axis=AX.X))
        st_done = ctx.sVE.v

        @block.tensor
        def _(t):
            wait(t, ctx.sVE, st_done)
            ctx.sPE.inc(t.transpose(out=ctx.p6[0:32, 0:128],
                                    in_=stat2[:], identity=ctx.idbf[:]))
        t1_done = ctx.sPE.v

        @block.vector
        def _(v):
            wait(v, ctx.sPE, t1_done)
            with nc.allow_low_precision(reason="softmax bound only; cancels"):
                ctx.sVE.inc(v.reduce_max(out=mm2[0:32, :], in_=ctx.p6[0:32, 0:128], axis=AX.X))
        mm2_done = ctx.sVE.v

        @block.tensor
        def _(t):
            wait(t, ctx.sVE, mm2_done)
            ctx.sPE.inc(t.transpose(out=ctx.p6[0:1, 512:544], in_=mm2[0:32, :],
                                    identity=ctx.idbf[0:32, 0:32]))
        t2_done = ctx.sPE.v

        @block.vector
        def _(v):
            wait(v, ctx.sPE, t2_done)
            ctx.sVE.inc(v.tensor_scalar(out=mrow[:], in0=ctx.p6[0:1, 512:544],
                                        scalar1=6e-20, scalar2=None, op0=ALU.max))
        clip_done = ctx.sVE.v

        @block.scalar
        def _(s):
            wait(s, ctx.sVE, clip_done)
            ctx.sSC.inc(s.activation(out=lrow[:], in_=mrow[:], func=AF.Ln))
        ln_done = ctx.sSC.v

        @block.vector
        def _(v):
            wait(v, ctx.sSC, ln_done)
            ctx.sVE.inc(v.tensor_tensor(out=urow[:], in0=lrow[0:1, 0:16],
                                        in1=lrow[0:1, 16:32], op=ALU.add))
        add_done = ctx.sVE.v

        @block.scalar
        def _(s):
            wait(s, ctx.sVE, add_done)
            s.activation(out=urow[:], in_=urow[:], func=AF.Exp, scale=0.5)
            ctx.sSC.inc(s.activation(out=urow[:], in_=urow[:], func=AF.Identity,
                                     scale=-0.125))
        ub_done = ctx.sSC.v

        @block.tensor
        def _(t):
            wait(t, ctx.sSC, ub_done)
            wait(t, ctx.sVE)
            ctx.sPE.inc(t.matmul(out=ctx.p7[:, 0:16], lhsT=ctx.ones128[:], rhs=urow[:],
                                 start=True, stop=True))
        ubc_done = ctx.sPE.v

        @block.vector
        def _(v):
            wait(v, ctx.sPE, ubc_done)
            ctx.sVE.inc(v.tensor_copy(out=ub_sb[:], in_=ctx.p7[:, 0:16]))
        ub_ready = ctx.sVE.v

        spill_done = ctx.sST.v
        kld = []

        def pl_of(kt):
            return ctx.p0 if kt % 2 == 0 else ctx.p1
        prev_exp_last = 0
        prev_exp_even = 0
        lnz_guard = 0
        prev_bcv_ve = 0
        gp_guard = ctx.sGP.v
        for c in range(NCH):
            @block.sync
            def _(sync):
                wait(sync, ctx.sST, spill_done)
                wait(sync, ctx.sPE)
                ctx.sIN.inc(sync.dma_start(
                    out=kt_sb[:, :, c * 512:(c + 1) * 512],
                    in_=kspill[:, :, c * 512:(c + 1) * 512]), 16)
                ctx.sIN.inc(sync.dma_start(out=qtc[:],
                                           in_=qspill[:, :, c * 512:(c + 1) * 512]), 16)
            kld.append(ctx.sIN.v)

            for jo in range(NH // 2):
                hA, hB = 2 * jo, 2 * jo + 1
                nkt = 4 * c + 4
                diag0 = 4 * c
                sSC_base = ctx.sSC.v
                expA = [sSC_base + 2 * kt + 1 for kt in range(nkt)]
                expB = [sSC_base + 2 * kt + 2 for kt in range(nkt)]
                rsA_proj = sSC_base + 2 * nkt + 1
                rsB_proj = sSC_base + 2 * nkt + 2
                sVE_all = ctx.sVE.v
                mask_projA = {dk: sVE_all + 2 * i + 1 for i, dk in enumerate(range(diag0, nkt))}
                mask_projB = {dk: sVE_all + 2 * i + 2 for i, dk in enumerate(range(diag0, nkt))}
                qk_act = []
                pv_act = []

                def pair_banks(kt):
                    if kt % 2 == 0:
                        return ctx.p0[:], ctx.p1[:]
                    return ctx.p45[:, 0:512], ctx.p45[:, 512:1024]

                @block.tensor
                def _(t):
                    def emit_pv(pv, waited):
                        pvoff = max(0, (pv - diag0) * 128)
                        if pv == 0:
                            wait(t, ctx.sVE, sVE_all)
                        if pv >= diag0:
                            wait_f(t, ctx.sVE, mask_projB[pv])
                        elif not waited:
                            wait_f(t, ctx.sSC, expB[pv])
                        t.matmul(out=ctx.p2[0:65, pvoff:512],
                                 lhsT=v_sb[:, pv, hA * 65:hA * 65 + 65],
                                 rhs=ptA[:, pv % 2, pvoff:512],
                                 start=(pv == 0), stop=(pv == nkt - 1),
                                 skip_group_check=True)
                        ctx.sPE.inc(t.matmul(
                            out=ctx.p3[0:65, pvoff:512],
                            lhsT=v_sb[:, pv, hB * 65:hB * 65 + 65],
                            rhs=ptB[:, pv % 2, pvoff:512],
                            start=(pv == 0), stop=(pv == nkt - 1),
                            skip_group_check=True))
                        pv_act.append(ctx.sPE.v)
                    for kt in range(nkt):
                        off = max(0, (kt - diag0) * 128)
                        if kt == 0:
                            wait(t, ctx.sIN, kld[c])
                            if c == 0 and jo == 0:
                                wait(t, ctx.sVE, sVE_all)
                            if prev_exp_even:
                                wait(t, ctx.sSC, prev_exp_even)
                        if kt == 1:
                            if prev_exp_last:
                                wait(t, ctx.sSC, prev_exp_last)
                            if jo == 0:
                                wait(t, ctx.sVE, lnz_guard)
                            if c == 0 and jo == 0:
                                wait(t, ctx.sGP, gp_guard)
                            if prev_bcv_ve:
                                wait(t, ctx.sVE, prev_bcv_ve)
                        if kt >= 2:
                            wait_f(t, ctx.sSC, expB[kt - 2])
                        plA, plB = pair_banks(kt)
                        t.matmul(out=plA[:, off:512],
                                 lhsT=kt_sb[0:64, jo, kt * 128:(kt + 1) * 128],
                                 rhs=qtc[0:64, jo, off:512],
                                 start=True, stop=True, skip_group_check=True)
                        ctx.sPE.inc(t.matmul(
                            out=plB[:, off:512],
                            lhsT=kt_sb[64:128, jo, kt * 128:(kt + 1) * 128],
                            rhs=qtc[64:128, jo, off:512],
                            start=True, stop=True, skip_group_check=True))
                        qk_act.append(ctx.sPE.v)
                        if kt >= 2:
                            emit_pv(kt - 2, waited=True)
                    emit_pv(nkt - 2, waited=False)
                    emit_pv(nkt - 1, waited=False)
                    if prev_bcv_ve:
                        wait(t, ctx.sVE, prev_bcv_ve)
                    wait_f(t, ctx.sSC, rsA_proj)
                    t.matmul(out=ctx.p7[0:64, :], lhsT=ctx.ones64[:],
                             rhs=rsA[:], start=True, stop=True,
                             skip_group_check=True)
                    wait_f(t, ctx.sSC, rsB_proj)
                    ctx.sPE.inc(t.matmul(out=ctx.p45[0:64, 0:512], lhsT=ctx.ones64[:],
                                         rhs=rsB[:], start=True, stop=True,
                                         skip_group_check=True))
                bc_act = ctx.sPE.v

                @block.vector
                def _(v):
                    wait(v, ctx.sIN, ctx.const_ready_in)
                    for kt in range(diag0, nkt):
                        off = max(0, (kt - diag0) * 128)
                        wait_f(v, ctx.sSC, expA[kt])
                        ctx.sVE.inc(v.tensor_tensor(
                            out=ptA[:, kt % 2, off:off + 128],
                            in0=ptA[:, kt % 2, off:off + 128],
                            in1=ctx.trit[:], op=ALU.mult))
                        assert ctx.sVE.v == mask_projA[kt]
                        wait_f(v, ctx.sSC, expB[kt])
                        ctx.sVE.inc(v.tensor_tensor(
                            out=ptB[:, kt % 2, off:off + 128],
                            in0=ptB[:, kt % 2, off:off + 128],
                            in1=ctx.trit[:], op=ALU.mult))
                        assert ctx.sVE.v == mask_projB[kt]
                    wait(v, ctx.sPE, pv_act[nkt - 1])
                    v.tensor_copy(out=scA, in_=ctx.p2[64:65, :])
                    ctx.sVE.inc(v.tensor_copy(out=scB, in_=ctx.p3[64:65, :]))
                    clip_act2 = ctx.sVE.v
                    wait(v, ctx.sPE, bc_act)
                    v.tensor_copy(out=bcvA[:], in_=ctx.p7[0:64, :])
                    v.tensor_copy(out=bcvB[:], in_=ctx.p45[0:64, 0:512])
                    v.tensor_tensor(out=ctxT[0:64, jo, :],
                                    in0=ctx.p2[0:64, :], in1=bcvA[:],
                                    op=ALU.mult)
                    ctx.sVE.inc(v.tensor_tensor(out=ctxT[64:128, jo, :],
                                                in0=ctx.p3[0:64, :], in1=bcvB[:],
                                                op=ALU.mult))
                clip_act = ctx.sVE.v - 1
                prev_bcv_ve = ctx.sVE.v

                @block.scalar
                def _(s):
                    for kt in range(nkt):
                        off = max(0, (kt - diag0) * 128)
                        wait(s, ctx.sPE, qk_act[kt])
                        if kt >= 2:
                            wait(s, ctx.sPE, pv_act[kt - 2])
                            if kt - 2 >= diag0:
                                wait(s, ctx.sVE, mask_projB[kt - 2])
                        plA, plB = pair_banks(kt)
                        ctx.sSC.inc(s.activation(out=ptA[:, kt % 2, off:512],
                                                 in_=plA[:, off:512],
                                                 func=AF.Exp, scale=0.125,
                                                 bias=ub_sb[:, hA:hA + 1]))
                        ctx.sSC.inc(s.activation(out=ptB[:, kt % 2, off:512],
                                                 in_=plB[:, off:512],
                                                 func=AF.Exp, scale=0.125,
                                                 bias=ub_sb[:, hB:hB + 1]))
                    wait(s, ctx.sVE, clip_act)
                    s.activation(out=rsA[:], in_=scA, func=AF.Ln)
                    ctx.sSC.inc(s.activation(out=rsA[:], in_=rsA[:],
                                             func=AF.Exp, scale=-1.0))
                    s.activation(out=rsB[:], in_=scB, func=AF.Ln)
                    ctx.sSC.inc(s.activation(out=rsB[:], in_=rsB[:],
                                             func=AF.Exp, scale=-1.0))
                assert ctx.sSC.v == rsB_proj
                prev_exp_last = expB[nkt - 1]
                prev_exp_even = expB[nkt - 2]

            for tt in range(4):
                g = c * 4 + tt
                @block.tensor
                def _(t):
                    wait(t, ctx.sVE)
                    for half in range(2):
                        for k in range(8):
                            i = t.matmul(out=ctx.p45[:, half * 512:(half + 1) * 512],
                                         lhsT=ctxT[:, k, tt * 128:(tt + 1) * 128],
                                         rhs=wo_bf[:, k, half * 512:(half + 1) * 512],
                                         start=(k == 0), stop=(k == 7))
                    ctx.sPE.inc(i)
                emit_ln(ctx, block, ctx.p45[:], x2_d, g, x3_d, g)
            lnz_guard = ctx.sVE.v

        emit_final_wait(ctx, block)
    return nc



def build_p3(nc, x3_d, wg_d, wu_d, wd_d, out_d, pfx="", cdram=None, dbg=None):
    IDIM = 2816
    NI = IDIM // 128
    with contextlib.ExitStack() as st:
        ctx = Ctx()
        common_setup(ctx, st, nc, want=("IDBF",), slim=True, nz=1, alloc_zsq=False,
                     alloc_xbf=False, pfx=pfx, cdram=cdram)
        ctx.cln12 = st.enter_context(nc.sbuf_tensor(pfx + "cln12", [128, 1], F32))
        wgT = st.enter_context(nc.sbuf_tensor(pfx + "wgT", [128, 8, IDIM], BF16))
        wuT = st.enter_context(nc.sbuf_tensor(pfx + "wuT", [128, 8, IDIM], BF16))
        wdT = st.enter_context(nc.sbuf_tensor(pfx + "wdT", [128, NI, 1024], BF16))
        qb = {}
        for nm, dt, w in (("A", F32, IDIM), ("Y", F32, IDIM), ("Z", F32, IDIM),
                          ("M", F32, IDIM // 16),
                          ("MC", F32, IDIM // 16), ("R12", F32, IDIM // 16),
                          ("HS", F32, IDIM // 16)):
            qb[nm] = st.enter_context(nc.sbuf_tensor(pfx + "q_" + nm, [128, w], dt))
        sgn = {"A": st.enter_context(nc.sbuf_tensor(pfx + "sgnA", [128, 1408], BF16)),
               "B": st.enter_context(nc.sbuf_tensor(pfx + "sgnB", [128, 1408], BF16))}
        r12d = {"A": st.enter_context(nc.sbuf_tensor(pfx + "r12A", [128, 88], F32)),
                "B": st.enter_context(nc.sbuf_tensor(pfx + "r12B", [128, 88], F32))}
        ctx.zsq = qb["Y"][:, 0:1024]

        block = st.enter_context(nc.Block())
        emit_const_loads(ctx, block)

        @block.vector
        def _(v):
            ctx.sVE.inc(v.memset(ctx.cln12[:], float(np.log(12.0))))

        S = {}
        src_of = {}

        def quant_p1(src_ap, L, qoff, soff, key, sg, src_wait=None):
            nb = L // 16
            SGb = sgn[sg][:, 0:L]
            M = qb["M"][:, soff:soff + nb]
            MC = qb["MC"][:, soff:soff + nb]
            R12 = r12d[sg][:, 0:nb]

            @block.vector
            def _(v):
                if src_wait is not None:
                    wait(v, src_wait[0], src_wait[1])
                wait(v, ctx.sSC, S.get((key, 'r12_sc'), 0))
                v.reduce_max(
                    out=M, in_=src_ap.rearrange("p (b w) -> p b w", w=16),
                    axis=AX.X, apply_absolute_value=True)
                v.scalar_tensor_tensor(out=qb["A"][:, qoff:qoff + L], in0=src_ap,
                                       scalar=-1.0, in1=src_ap,
                                       op0=ALU.mult, op1=ALU.max)
                ctx.sVE.inc(v.tensor_scalar(out=MC, in0=M, scalar1=6e-12,
                                            scalar2=None, op0=ALU.max))
            rm_ve = ctx.sVE.v
            S[(key, 'rm_ve')] = rm_ve
            src_of[key] = src_ap

            @block.scalar
            def _(s):
                if src_wait is not None:
                    wait(s, src_wait[0], src_wait[1])
                wait(s, ctx.sVE, S.get(('sgn', sg), 0))
                s.activation(out=SGb, in_=src_ap, func=AF.Sign)
                wait(s, ctx.sVE, rm_ve)
                s.activation(out=R12, in_=MC, func=AF.Ln)
                ctx.sSC.inc(s.activation(out=R12, in_=R12, func=AF.Exp,
                                         scale=-1.0, bias=ctx.cln12[:]))
            S[(key, 'r12_sc')] = ctx.sSC.v

        def quant_p2(out_ap, L, qoff, soff, key, sg, out_war_te=0):
            nb = L // 16
            A = qb["A"][:, qoff:qoff + L]
            Y = qb["Y"][:, qoff:qoff + L]
            Z = qb["Z"][:, qoff:qoff + L]
            SGb = sgn[sg][:, 0:L]
            MC = qb["MC"][:, soff:soff + nb]
            R12 = r12d[sg][:, 0:nb]
            HS = qb["HS"][:, soff:soff + nb]
            r12_sc = S[(key, 'r12_sc')]

            @block.vector
            def _(v):
                wait(v, ctx.sSC, r12_sc)
                v.tensor_scalar(out=HS, in0=MC, scalar1=1.0 / 12.0,
                                scalar2=None, op0=ALU.mult)
                v.tensor_tensor(
                    out=Y.rearrange("p (b w) -> p b w", w=16),
                    in0=A.rearrange("p (b w) -> p b w", w=16),
                    in1=R12.unsqueeze(2).broadcast_to([128, nb, 16]),
                    op=ALU.mult)
                y, z, a = Y, Z, A
                v.tensor_scalar(out=z, in0=y, scalar1=0.5, scalar2=None, op0=ALU.is_gt)
                v.scalar_tensor_tensor(out=z, in0=y, scalar=1.5, in1=z, op0=ALU.is_gt, op1=ALU.add)
                v.scalar_tensor_tensor(out=z, in0=y, scalar=2.5, in1=z, op0=ALU.is_gt, op1=ALU.add)
                v.scalar_tensor_tensor(out=z, in0=y, scalar=3.5, in1=z, op0=ALU.is_gt, op1=ALU.add)
                v.tensor_scalar(out=a, in0=y, scalar1=5.0, scalar2=None, op0=ALU.is_gt)
                v.scalar_tensor_tensor(out=a, in0=y, scalar=7.0, in1=a, op0=ALU.is_gt, op1=ALU.add)
                v.scalar_tensor_tensor(out=a, in0=y, scalar=10.0, in1=a, op0=ALU.is_gt, op1=ALU.add)
                v.scalar_tensor_tensor(out=a, in0=y, scalar=10.0, in1=a, op0=ALU.is_gt, op1=ALU.add)
                v.scalar_tensor_tensor(out=z, in0=a, scalar=2.0, in1=z, op0=ALU.mult, op1=ALU.add)
                v.tensor_tensor(out=z, in0=z, in1=SGb, op=ALU.mult)
                if out_war_te:
                    wait(v, ctx.sPE, out_war_te)
                ctx.sVE.inc(v.tensor_tensor(
                    out=out_ap.rearrange("p (b w) -> p b w", w=16),
                    in0=Z.rearrange("p (b w) -> p b w", w=16),
                    in1=HS.unsqueeze(2).broadcast_to([128, nb, 16]), op=ALU.mult))
            S[(key, 'ladder_ve')] = ctx.sVE.v
            S[('sgn', sg)] = ctx.sVE.v

        def wtr_evac(src_bf, j0, jn, dst_ap, lad_ve, tr_war_ve):
            @block.tensor
            def _(t):
                wait(t, ctx.sVE, lad_ve)
                wait(t, ctx.sVE, tr_war_ve)
                wait(t, ctx.sIN, ctx.const_ready_in)
                for j in range(jn):
                    i = t.transpose(out=ctx.p6[:, j * 128:(j + 1) * 128],
                                    in_=src_bf[:, (j0 + j) * 128:(j0 + j + 1) * 128],
                                    identity=ctx.idbf[:])
                ctx.sPE.inc(i)
            tr_te = ctx.sPE.v

            @block.vector
            def _(v):
                wait(v, ctx.sPE, tr_te)
                ctx.sVE.inc(v.tensor_copy(
                    out=dst_ap,
                    in_=ctx.p6[:, 0:jn * 128].rearrange("p (j t) -> p j t", j=jn)))
            S[('wevac_ve',)] = ctx.sVE.v
            return tr_te

        wq_cm = nc.sbuf_tensor(pfx + "wq_bf", [128, IDIM], BF16)
        wq_bf = wq_cm.__enter__()
        pend = None
        def flush_w():
            nonlocal pend
            if pend is None:
                return
            slot, qoff, dstT, r = pend
            quant_p2(wq_bf[:, qoff:qoff + 1024], 1024, qoff, slot * 64,
                     ('w', slot), 'A' if slot == 0 else 'B',
                     out_war_te=S.get(('wtr_te', slot), 0))
            S[('xin', slot, 've')] = S[(('w', slot), 'ladder_ve')]
            S[('wtr_te', slot)] = wtr_evac(
                wq_bf[:, qoff:qoff + 1024], 0, 8,
                dstT[:, 0:8, r * 128:r * 128 + 128],
                S[(('w', slot), 'ladder_ve')], S.get(('wevac_ve',), 0))
            pend = None

        for w_d, dstT in ((wg_d, wgT), (wu_d, wuT)):
            for r in range(NI):
                slot = r % 2
                qoff = slot * 1024
                flush_w()

                @block.sync
                def _(sync):
                    wait(sync, ctx.sVE, S.get(('xin', slot, 've'), 0))
                    wait(sync, ctx.sSC, S.get(('xin', slot, 'sc'), 0))
                    ctx.sIN.inc(sync.dma_start(out=ctx.xin[:, slot, :],
                                               in_=w_d[r * 128:(r + 1) * 128, :]), 16)
                ld = ctx.sIN.v
                quant_p1(ctx.xin[:, slot, :], 1024, qoff, slot * 64,
                         ('w', slot), 'A' if slot == 0 else 'B',
                         src_wait=(ctx.sIN, ld))
                S[('xin', slot, 'sc')] = S[(('w', slot), 'r12_sc')]
                pend = (slot, qoff, dstT, r)
        flush_w()
        wgwu_bar = S[('wevac_ve',)]
        for sl in (0, 1):
            S[(('wd', sl), 'ladder_ve')] = ctx.sVE.v
            S[(('wd', sl), 'rm_ve')] = ctx.sVE.v
            S[(('wd', sl), 'r12_sc')] = ctx.sSC.v

        HW = 1408
        pendd = None
        def flush_wd():
            nonlocal pendd
            if pendd is None:
                return
            slot, qoff, r, hh = pendd
            quant_p2(wq_bf[:, qoff:qoff + HW], HW, qoff, slot * 88,
                     ('wd', slot), 'A' if slot == 0 else 'B',
                     out_war_te=S.get(('wdtr_te', slot), 0))
            S[('wdsrc', 've')] = S[(('wd', slot), 'ladder_ve')]
            lad = S[(('wd', slot), 'ladder_ve')]
            src = wq_bf[:, qoff:qoff + HW]
            wtr_evac(src, 0, 8,
                     wdT[:, hh * 11:hh * 11 + 8, r * 128:(r + 1) * 128],
                     lad, S.get(('wevac_ve',), 0))
            t2 = wtr_evac(src, 8, 3,
                          wdT[:, hh * 11 + 8:hh * 11 + 11, r * 128:(r + 1) * 128],
                          lad, S.get(('wevac_ve',), 0))
            S[('wdtr_te', slot)] = t2
            pendd = None

        with nc.sbuf_tensor(pfx + "wstage", [128, HW], F32) as wstage:
            for rr in range(16):
                r, hh = divmod(rr, 2)
                slot = rr % 2
                qoff = slot * HW
                flush_wd()

                @block.sync
                def _(sync):
                    wait(sync, ctx.sVE, S.get(('wdsrc', 've'), 0))
                    wait(sync, ctx.sSC, S.get(('wdsrc', 'sc'), 0))
                    ctx.sIN.inc(sync.dma_start(
                        out=wstage[:],
                        in_=wd_d[r * 128:(r + 1) * 128, hh * HW:(hh + 1) * HW]), 16)
                ld = ctx.sIN.v
                quant_p1(wstage[:], HW, qoff, slot * 88,
                         ('wd', slot), 'A' if slot == 0 else 'B',
                         src_wait=(ctx.sIN, ld))
                S[('wdsrc', 'sc')] = S[(('wd', slot), 'r12_sc')]
                pendd = (slot, qoff, r, hh)
            flush_wd()
        wd_bar = S[('wevac_ve',)]
        wq_cm.__exit__(None, None, None)

        if dbg is not None:
            @block.sync
            def _(sync):
                wait(sync, ctx.sVE)
                ctx.sIN.inc(sync.dma_start(out=dbg['WGT'], in_=wgT[:]), 16)
                ctx.sIN.inc(sync.dma_start(out=dbg['WUT'], in_=wuT[:]), 16)
                ctx.sIN.inc(sync.dma_start(out=dbg['WDT'], in_=wdT[:]), 16)
            _dbg_ld = ctx.sIN.v

            @block.scalar
            def _(s):
                wait(s, ctx.sIN, _dbg_ld)

        for key in (('h', 0), ('h', 1), ('xq',)):
            S[(key, 'ladder_ve')] = ctx.sVE.v
            S[(key, 'rm_ve')] = ctx.sVE.v
            S[(key, 'r12_sc')] = ctx.sSC.v
        S[('evac_ve', 0)] = S[('evac_ve', 1)] = wd_bar

        if dbg is not None:
            dbg['G1SB'] = st.enter_context(nc.sbuf_tensor(pfx + "dbgsb", [128, 512], F32))
        xq_bf = st.enter_context(nc.sbuf_tensor(pfx + "xq_bf", [128, 1024], BF16))
        xqT0 = st.enter_context(nc.sbuf_tensor(pfx + "xqT", [128, 8, 128], BF16))
        hbuf = st.enter_context(nc.sbuf_tensor(pfx + "hbuf", [128, 512], F32))
        hq_bf = st.enter_context(nc.sbuf_tensor(pfx + "hq_bf", [128, 512], BF16))
        xqT = [xqT0[:],
               qb["A"][:, 2048:2560].bitcast(BF16).rearrange("p (j t) -> p j t", j=8)]
        hqs = [qb["Y"][:, 2560:2816].bitcast(BF16), hq_bf[:, 0:512]]
        h2s = [qb["Z"][:, 2048:2560], hbuf[:, 0:512]]
        hqT = [qb["Y"][:, 2048:2304].bitcast(BF16).rearrange("p (j t) -> p j t", j=4),
               qb["Y"][:, 2304:2560].bitcast(BF16).rearrange("p (j t) -> p j t", j=4)]

        CW = [512] * 5 + [256]
        IC0 = [0, 512, 1024, 1536, 2048, 2560]

        def emit_xdma(g):
            slot = g % 2

            @block.sync
            def _(sync):
                wait(sync, ctx.sVE, S.get(('xin', slot, 've'), 0))
                wait(sync, ctx.sSC, S.get(('xin', slot, 'sc'), 0))
                ctx.sIN.inc(sync.dma_start(out=ctx.xin[:, slot, :],
                                           in_=x3_d[g * 128:(g + 1) * 128, :]), 16)
            S[('xdma', g % 2)] = ctx.sIN.v

        def emit_xquant1(g):
            slot = g % 2
            quant_p1(ctx.xin[:, slot, :], 1024, 0, 0, ('xq',), 'A',
                     src_wait=(ctx.sIN, S[('xdma', slot)]))
            S[('xin', slot, 'sc')] = S[(('xq',), 'r12_sc')]

        def emit_xquant2(g):
            quant_p2(xq_bf[:], 1024, 0, 0, ('xq',), 'A',
                     out_war_te=S.get(('xtr_te',), 0))
            S[('xin', g % 2, 've')] = S[(('xq',), 'ladder_ve')]

        def emit_xtr_evac(g):
            slot = g % 2
            if dbg is not None and 'XQ' in dbg:
                @block.sync
                def _(sync):
                    wait(sync, ctx.sVE, S[(('xq',), 'ladder_ve')])
                    ctx.sIN.inc(sync.dma_start(
                        out=dbg['XQ'][g * 128:(g + 1) * 128, :], in_=xq_bf[:]), 16)
                _ldx = ctx.sIN.v

                @block.vector
                def _(v):
                    wait(v, ctx.sIN, _ldx)

            @block.tensor
            def _(t):
                wait(t, ctx.sVE, S[(('xq',), 'ladder_ve')])
                wait(t, ctx.sVE, S[('evac_ve', 0)])
                wait(t, ctx.sVE, S[('evac_ve', 1)])
                wait(t, ctx.sIN, ctx.const_ready_in)
                for j in range(8):
                    i = t.transpose(out=ctx.p6[:, j * 128:(j + 1) * 128],
                                    in_=xq_bf[:, j * 128:(j + 1) * 128],
                                    identity=ctx.idbf[:])
                ctx.sPE.inc(i)
            S[('xtr_te',)] = ctx.sPE.v

            @block.vector
            def _(v):
                wait(v, ctx.sPE, S[('xtr_te',)])
                wait(v, ctx.sPE, S.get(('gu_last_te', slot), 0))
                ctx.sVE.inc(v.tensor_copy(
                    out=xqT[slot][:],
                    in_=ctx.p6[:].rearrange("p (j t) -> p j t", j=8)))
            S[('xevac_ve', slot)] = ctx.sVE.v
            S[('evac_ve', 0)] = S[('evac_ve', 1)] = ctx.sVE.v

        def emit_gu(g, ic):
            p = ic % 2
            pg0 = ctx.p0 if p == 0 else ctx.p2
            pg1 = ctx.p1 if p == 0 else ctx.p3
            cw, ic0 = CW[ic], IC0[ic]

            @block.tensor
            def _(t):
                if ic == 0:
                    wait(t, ctx.sVE, S[('xevac_ve', g % 2)])
                    if g == 0:
                        wait(t, ctx.sVE, wgwu_bar)
                wait(t, ctx.sSC, S.get(('silu_sc', p), 0))
                wait(t, ctx.sVE, S.get(('hmult_ve', p), 0))
                for k in range(8):
                    t.matmul(out=pg0[:, 0:cw], lhsT=xqT[g % 2][:, k, :],
                             rhs=wgT[:, k, ic0:ic0 + cw],
                             start=(k == 0), stop=(k == 7), skip_group_check=True)
                for k in range(8):
                    i = t.matmul(out=pg1[:, 0:cw], lhsT=xqT[g % 2][:, k, :],
                                 rhs=wuT[:, k, ic0:ic0 + cw],
                                 start=(k == 0), stop=(k == 7), skip_group_check=True)
                ctx.sPE.inc(i)
            S[('gu_te', p)] = ctx.sPE.v
            if ic == 5:
                S[('gu_last_te', g % 2)] = ctx.sPE.v

        def emit_silu_mult(g, ic):
            p = ic % 2
            pg0 = ctx.p0 if p == 0 else ctx.p2
            pg1 = ctx.p1 if p == 0 else ctx.p3
            cw = CW[ic]
            gu_te = S[('gu_te', p)]
            h2 = h2s[p][:, 0:cw]
            if dbg is not None and 'G1' in dbg and g == 1:
                @block.vector
                def _(v):
                    wait(v, ctx.sPE, gu_te)
                    ctx.sVE.inc(v.tensor_copy(out=dbg['G1SB'][:, 0:cw],
                                              in_=pg0[:, 0:cw]))
                _cp = ctx.sVE.v

                @block.sync
                def _(sync):
                    wait(sync, ctx.sVE, _cp)
                    ctx.sIN.inc(sync.dma_start(
                        out=dbg['G1'][:, IC0[ic]:IC0[ic] + cw],
                        in_=dbg['G1SB'][:, 0:cw]), 16)
                _ldg = ctx.sIN.v

                @block.vector
                def _(v):
                    wait(v, ctx.sIN, _ldg)

            @block.scalar
            def _(s):
                wait(s, ctx.sPE, gu_te)
                wait(s, ctx.sVE, S.get((('h', p), 'ladder_ve'), 0))
                s.activation(out=h2, in_=pg0[:, 0:cw], func=AF.Exp, scale=-1.0)
                s.activation(out=h2, in_=h2, func=AF.Ln, bias=ctx.cone[:])
                ctx.sSC.inc(s.activation(out=h2, in_=h2, func=AF.Exp, scale=-1.0))
            S[('silu_sc', p)] = ctx.sSC.v

            @block.vector
            def _(v):
                wait(v, ctx.sSC, S[('silu_sc', p)])
                v.tensor_tensor(out=h2, in0=pg0[:, 0:cw], in1=h2, op=ALU.mult)
                ctx.sVE.inc(v.tensor_tensor(out=h2, in0=pg1[:, 0:cw], in1=h2,
                                            op=ALU.mult))
            S[('hmult_ve', p)] = ctx.sVE.v

        def emit_quant_h1(g, ic):
            p = ic % 2
            cw = CW[ic]
            if dbg is not None and 'H21' in dbg and g == 1:
                @block.vector
                def _(v):
                    ctx.sVE.inc(v.tensor_copy(out=dbg['G1SB'][:, 0:cw],
                                              in_=h2s[p][:, 0:cw]))
                _cp = ctx.sVE.v

                @block.sync
                def _(sync):
                    wait(sync, ctx.sVE, _cp)
                    ctx.sIN.inc(sync.dma_start(
                        out=dbg['H21'][:, IC0[ic]:IC0[ic] + cw],
                        in_=dbg['G1SB'][:, 0:cw]), 16)
                _ldh = ctx.sIN.v

                @block.vector
                def _(v):
                    wait(v, ctx.sIN, _ldh)
            quant_p1(h2s[p][:, 0:cw], cw, 1024 + 512 * p, 64 + 32 * p, ('h', p),
                     'A' if p == 0 else 'B',
                     src_wait=(ctx.sVE, S[('hmult_ve', p)]))

        def emit_quant_h2(g, ic):
            p = ic % 2
            cw = CW[ic]
            quant_p2(hqs[p][:, 0:cw], cw, 1024 + 512 * p, 64 + 32 * p, ('h', p),
                     'A' if p == 0 else 'B',
                     out_war_te=S.get(('tr_te', p), 0))
            if dbg is not None and 'HQ1' in dbg and g == 1:
                lad = S[(('h', p), 'ladder_ve')]

                @block.sync
                def _(sync):
                    wait(sync, ctx.sVE, lad)
                    ctx.sIN.inc(sync.dma_start(
                        out=dbg['HQ1'][:, IC0[ic]:IC0[ic] + cw],
                        in_=hqs[p][:, 0:cw]), 16)
                _ld = ctx.sIN.v

                @block.vector
                def _(v):
                    wait(v, ctx.sIN, _ld)

        def emit_trdown(g, ic):
            p = ic % 2
            nkk = CW[ic] // 128

            @block.tensor
            def _(t):
                wait(t, ctx.sVE, S[(('h', p), 'ladder_ve')])
                wait(t, ctx.sVE, S[('evac_ve', p)])
                for kk in range(nkk):
                    i = t.transpose(out=ctx.p6[:, p * 512 + kk * 128:p * 512 + (kk + 1) * 128],
                                    in_=hqs[p][:, kk * 128:(kk + 1) * 128],
                                    identity=ctx.idbf[:])
                ctx.sPE.inc(i)
            tr_te = ctx.sPE.v
            S[('tr_te', p)] = tr_te

            @block.vector
            def _(v):
                wait(v, ctx.sPE, tr_te)
                wait(v, ctx.sPE, S.get(('down_te', p), 0))
                ctx.sVE.inc(v.tensor_copy(
                    out=hqT[p][:, 0:nkk, :],
                    in_=ctx.p6[:, p * 512:p * 512 + nkk * 128].rearrange(
                        "p (j t) -> p j t", j=nkk)))
            S[('evac_ve', p)] = ctx.sVE.v

            @block.tensor
            def _(t):
                wait(t, ctx.sVE, S[('evac_ve', p)])
                if ic == 0:
                    wait(t, ctx.sVE, S.get(('lnz_ve',), 0))
                    if g == 0:
                        wait(t, ctx.sVE, wd_bar)
                for kk in range(nkk):
                    for half in range(2):
                        i = t.matmul(out=ctx.p45[:, half * 512:(half + 1) * 512],
                                     lhsT=hqT[p][:, kk, :],
                                     rhs=wdT[:, ic * 4 + kk, half * 512:(half + 1) * 512],
                                     start=(ic == 0 and kk == 0),
                                     stop=(ic == 5 and kk == nkk - 1),
                                     skip_group_check=True)
                ctx.sPE.inc(i)
            S[('down_te', p)] = ctx.sPE.v

        emit_xdma(0)
        emit_xquant1(0)
        emit_xquant2(0)
        emit_xtr_evac(0)
        for g in range(NT):
            for ic in range(6):
                s_ = 6 * g + ic
                emit_gu(g, ic)
                if s_ >= 2:
                    g2, ic2 = divmod(s_ - 2, 6)
                    emit_trdown(g2, ic2)
                    if ic2 == 5:
                        emit_ln(ctx, block, ctx.p45[:], x3_d, g2, out_d, g2)
                        S[('lnz_ve',)] = ctx.sVE.v
                if s_ >= 1:
                    g1, ic1 = divmod(s_ - 1, 6)
                    emit_quant_h2(g1, ic1)
                if g + 1 < NT:
                    if ic == 2:
                        emit_xdma(g + 1)
                    elif ic == 3:
                        emit_xquant1(g + 1)
                    elif ic == 4:
                        emit_xquant2(g + 1)
                emit_silu_mult(g, ic)
                emit_quant_h1(g, ic)
                if g + 1 < NT and ic == 5:
                    emit_xtr_evac(g + 1)
        g1, ic1 = divmod(6 * NT - 1, 6)
        emit_quant_h2(g1, ic1)
        for s_ in (6 * NT - 2, 6 * NT - 1):
            g2, ic2 = divmod(s_, 6)
            emit_trdown(g2, ic2)
            if ic2 == 5:
                emit_ln(ctx, block, ctx.p45[:], x3_d, g2, out_d, g2)


        emit_final_wait(ctx, block)
    return nc


from concourse.bass_utils import run_bass_kernel_spmd

_CONSTS = None
_FUSED = None


def _consts():
    global _CONSTS
    if _CONSTS is None:
        _CONSTS = host_consts()
    return _CONSTS


def _fused_prog():
    global _FUSED
    if _FUSED is not None:
        return _FUSED
    nc = bass.Bass("TRN2", target_bir_lowering=False)
    x0 = nc.dram_tensor("X0", [SEQ, DM], F32, kind="ExternalInput").ap()
    wq = nc.dram_tensor("WQ", [1024, 512], F32, kind="ExternalInput").ap()
    wkv = nc.dram_tensor("WKV", [1024, 1024], F32, kind="ExternalInput").ap()
    wo = nc.dram_tensor("WO", [512, 1024], F32, kind="ExternalInput").ap()
    bw = nc.dram_tensor("BW", [1024, 1024], F32, kind="ExternalInput").ap()
    bb = nc.dram_tensor("BB", [1024], F32, kind="ExternalInput").ap()
    la = nc.dram_tensor("LA", [1024, 8], F32, kind="ExternalInput").ap()
    lb = nc.dram_tensor("LB", [8, 8, 1024], F32, kind="ExternalInput").ap()
    rw = nc.dram_tensor("RW", [8, 1024], F32, kind="ExternalInput").ap()
    rb = nc.dram_tensor("RB", [8], F32, kind="ExternalInput").ap()
    wqkv = nc.dram_tensor("WQKV", [1024, 3072], F32, kind="ExternalInput").ap()
    wo1 = nc.dram_tensor("WO1", [1024, 1024], F32, kind="ExternalInput").ap()
    wg = nc.dram_tensor("WG", [2816, 1024], F32, kind="ExternalInput").ap()
    wu = nc.dram_tensor("WU", [2816, 1024], F32, kind="ExternalInput").ap()
    wd = nc.dram_tensor("WD", [1024, 2816], F32, kind="ExternalInput").ap()
    cdram = {
        "IDBF": nc.dram_tensor("IDBF", [128, 128], BF16, kind="ExternalInput").ap(),
        "IDF32": nc.dram_tensor("IDF32", [128, 128], F32, kind="ExternalInput").ap(),
        "HSEL": nc.dram_tensor("HSEL", [128, 2], F32, kind="ExternalInput").ap(),
        "HSELT": nc.dram_tensor("HSELT", [2, 128], F32, kind="ExternalInput").ap(),
        "TRIT": nc.dram_tensor("TRIT", [128, 128], BF16, kind="ExternalInput").ap(),
        "ROPE_COS": nc.dram_tensor("ROPE_COS", [128, NT, 32], F32, kind="ExternalInput").ap(),
        "ROPE_SIN": nc.dram_tensor("ROPE_SIN", [128, NT, 32], F32, kind="ExternalInput").ap(),
    }
    out = nc.dram_tensor("OUT", [SEQ, DM], F32, kind="ExternalOutput").ap()
    x1 = nc.dram_tensor("X1i", [SEQ, DM], F32, kind="Internal").ap()
    x2 = nc.dram_tensor("X2i", [SEQ, DM], F32, kind="Internal").ap()
    x3 = nc.dram_tensor("X3i", [SEQ, DM], F32, kind="Internal").ap()
    qs = nc.dram_tensor("QSPILL", [128, 8, SEQ], BF16, kind="Internal").ap()
    ks = nc.dram_tensor("KSPILL", [128, 8, SEQ], BF16, kind="Internal").ap()
    build_p0(nc, x0, wq, wkv, wo, x1, pfx="a_", cdram=cdram)
    build_p1(nc, x1, bw, bb, la, lb, rw, rb, x2, pfx="b_", cdram=cdram)
    build_p2(nc, x2, wqkv, wo1, x3, qs, ks, pfx="c_", cdram=cdram)
    build_p3(nc, x3, wg, wu, wd, out, pfx="d_", cdram=cdram)
    _FUSED = nc
    return nc


def kernel(L0_attn_x, L0_attn_Wq, L0_attn_Wkv, L0_attn_Wo, L0_ffn_base_w,
           L0_ffn_base_b, L0_ffn_lora_A, L0_ffn_lora_B, L0_ffn_router_w,
           L0_ffn_router_b, L1_attn_wqkv, L1_attn_wo, L1_ffn_w_gate,
           L1_ffn_w_up, L1_ffn_w_down):
    NCORES = 8
    c = _consts()
    x = np.ascontiguousarray(np.asarray(L0_attn_x, dtype=np.float32))
    base = {"WQ": L0_attn_Wq, "WKV": L0_attn_Wkv, "WO": L0_attn_Wo,
            "BW": L0_ffn_base_w, "BB": L0_ffn_base_b, "LA": L0_ffn_lora_A,
            "LB": L0_ffn_lora_B, "RW": L0_ffn_router_w, "RB": L0_ffn_router_b,
            "WQKV": L1_attn_wqkv, "WO1": L1_attn_wo, "WG": L1_ffn_w_gate,
            "WU": L1_ffn_w_up, "WD": L1_ffn_w_down,
            "IDBF": c["IDBF"], "IDF32": c["IDF32"], "HSEL": c["HSEL"],
            "HSELT": c["HSELT"], "TRIT": c["TRIT"],
            "ROPE_COS": c["ROPE_COS"], "ROPE_SIN": c["ROPE_SIN"]}
    res = run_bass_kernel_spmd(_fused_prog(),
                               [dict(base, X0=x[b]) for b in range(NCORES)],
                               core_ids=list(range(NCORES)))
    out = np.stack([np.asarray(res.results[b]["OUT"]) for b in range(NCORES)], axis=0)
    return out.astype(np.float32)


# version 20 (final)
# speedup vs baseline: 1.0118x; 1.0118x over previous
import contextlib
import numpy as np
import ml_dtypes

import concourse.bass as bass
import concourse.mybir as mybir

F32 = mybir.dt.float32
BF16 = mybir.dt.bfloat16
AF = mybir.ActivationFunctionType
ALU = mybir.AluOpType
AX = mybir.AxisListType

SEQ = 4096
DM = 1024
NT = SEQ // 128
NCH = SEQ // 512
KD = DM // 128


class Sem:
    def __init__(self, st, nc, name):
        self.h = st.enter_context(nc.semaphore(name))
        self.v = 0

    def inc(self, instr, n=1):
        instr.then_inc(self.h, n)
        self.v += n
        return self.v


def wait(eng, sem, val=None):
    v = sem.v if val is None else val
    assert v <= sem.v, f"forward wait {v} > {sem.v}"
    if v > 0:
        eng.wait_ge(sem.h, v)


def wait_f(eng, sem, val):
    if val > 0:
        eng.wait_ge(sem.h, val)


class Ctx:
    pass


def host_consts():
    c = {}
    c["IDBF"] = np.eye(128, dtype=ml_dtypes.bfloat16)
    c["HSELT"] = np.stack([np.arange(128) < 64, np.arange(128) >= 64]).astype(np.float32)
    c["HSEL"] = np.ascontiguousarray(c["HSELT"].T)
    c["IDF32"] = np.eye(128, dtype=np.float32)
    jj, ii = np.meshgrid(np.arange(128), np.arange(128), indexing="ij")
    c["TRIT"] = (jj <= ii).astype(ml_dtypes.bfloat16)
    t = np.arange(SEQ, dtype=np.float64)
    cgrid = np.arange(32, dtype=np.float64)
    ts = 10000.0 ** (2.0 * cgrid / 64.0)
    ang = t[:, None] / ts[None, :]
    cosv = np.cos(ang).astype(np.float32).reshape(NT, 128, 32).transpose(1, 0, 2)
    sinv = np.sin(ang).astype(np.float32).reshape(NT, 128, 32).transpose(1, 0, 2)
    c["ROPE_COS"] = np.ascontiguousarray(cosv)
    c["ROPE_SIN"] = np.ascontiguousarray(sinv)
    return c


def common_setup(ctx, st, nc, want=("IDBF", "HSEL", "HSELT"), slim=False, nz=2,
                 alloc_zsq=True, alloc_xbf=True, pfx="", cdram=None):
    ctx.nc = nc
    ctx.st = st
    ctx.pfx = pfx

    def SB(name, shape, dt):
        return st.enter_context(nc.sbuf_tensor(pfx + name, shape, dt))
    ctx.SB = SB
    ctx.sIN = Sem(st, nc, pfx + "sIN")
    ctx.sPE = Sem(st, nc, pfx + "sPE")
    ctx.sVE = Sem(st, nc, pfx + "sVE")
    ctx.sSC = Sem(st, nc, pfx + "sSC")
    ctx.sGP = Sem(st, nc, pfx + "sGP")
    ctx.sST = Sem(st, nc, pfx + "sST")
    ctx.p0 = st.enter_context(nc.psum_tensor(pfx + "p0", [128, 512], F32))
    ctx.p1 = st.enter_context(nc.psum_tensor(pfx + "p1", [128, 512], F32))
    ctx.p2 = st.enter_context(nc.psum_tensor(pfx + "p2", [128, 512], F32))
    ctx.p3 = st.enter_context(nc.psum_tensor(pfx + "p3", [128, 512], F32))
    ctx.p45 = st.enter_context(nc.psum_tensor(pfx + "p45", [128, 1024], F32))
    ctx.p6 = st.enter_context(nc.psum_tensor(pfx + "p6", [128, 1024], BF16))
    ctx.p7 = st.enter_context(nc.psum_tensor(pfx + "p7", [128, 512], F32))
    ctx.dram = {}

    def const_in(name, shape, dt):
        if cdram is not None:
            d = cdram[name]
        else:
            d = nc.dram_tensor(name, shape, dt, kind="ExternalInput").ap()
        sb = st.enter_context(nc.sbuf_tensor(pfx + name.lower() + "_sb", shape, dt))
        ctx.dram[name] = (sb, d)
        return sb
    if "IDBF" in want:
        ctx.idbf = const_in("IDBF", [128, 128], BF16)
    if "HSEL" in want:
        ctx.hsel = const_in("HSEL", [128, 2], F32)
    if "HSELT" in want:
        ctx.hselt = const_in("HSELT", [2, 128], F32)
    if "TRIT" in want:
        ctx.trit = const_in("TRIT", [128, 128], BF16)
    if "ROPE_COS" in want:
        ctx.ropec = const_in("ROPE_COS", [128, NT, 32], F32)
        ctx.ropes = const_in("ROPE_SIN", [128, NT, 32], F32)
    if "IDF32" in want:
        ctx.idf32 = const_in("IDF32", [128, 128], F32)
    ctx.cneg16 = st.enter_context(nc.sbuf_tensor(pfx + "cneg16", [128, 1], F32))
    ctx.cone = st.enter_context(nc.sbuf_tensor(pfx + "cone", [128, 1], F32))
    ctx.ceps = st.enter_context(nc.sbuf_tensor(pfx + "ceps", [128, 1], F32))
    ctx.ones64 = st.enter_context(nc.sbuf_tensor(pfx + "ones64", [1, 64], F32))
    ctx.ones128 = st.enter_context(nc.sbuf_tensor(pfx + "ones128", [1, 128], F32))
    ctx.nring = 2 if slim else 3
    ctx.nres = 1 if slim else 3
    ctx.nout = 1 if slim else 2
    ctx.xin = st.enter_context(nc.sbuf_tensor(pfx + "xin", [128, ctx.nring, 1024], F32))
    ctx.xres = st.enter_context(nc.sbuf_tensor(pfx + "xres", [128, ctx.nres, 1024], F32))
    if alloc_xbf:
        ctx.xbf = st.enter_context(nc.sbuf_tensor(pfx + "xbf", [128, 2, 1024], BF16))
    ctx.nz = nz
    ctx.z_sb = st.enter_context(nc.sbuf_tensor(pfx + "z_sb", [128, nz, 1024], F32))
    if alloc_zsq:
        ctx.zsq = st.enter_context(nc.sbuf_tensor(pfx + "zsq", [128, 1024], F32))
    ctx.x1o = st.enter_context(nc.sbuf_tensor(pfx + "x1o", [128, ctx.nout, 1024], F32))
    ctx.lnst = st.enter_context(nc.sbuf_tensor(pfx + "lnst", [128, 8], F32))


def emit_const_loads(ctx, block):
    @block.sync
    def _(sync):
        for name, (sb, d) in ctx.dram.items():
            ctx.sIN.inc(sync.dma_start(out=sb[:], in_=d), 16)

    @block.vector
    def _(v):
        v.memset(ctx.cneg16[:], -16.0)
        v.memset(ctx.cone[:], 1.0)
        v.memset(ctx.ceps[:], 1e-5)
        v.memset(ctx.ones64[:], 1.0)
        ctx.sVE.inc(v.memset(ctx.ones128[:], 1.0))
    ctx.const_ready_ve = ctx.sVE.v
    ctx.const_ready_in = ctx.sIN.v


def emit_weight_load(ctx, block, dst_bf, src_dram, ncols, ktiles, ring_base=0):
    for k in range(ktiles):
        @block.sync
        def _(sync):
            wait(sync, ctx.sVE)
            ctx.sIN.inc(sync.dma_start(
                out=ctx.xin[:, (ring_base + k) % ctx.nring, 0:ncols],
                in_=src_dram[k * 128:(k + 1) * 128, 0:ncols]), 16)
        ld = ctx.sIN.v

        @block.vector
        def _(v):
            wait(v, ctx.sIN, ld)
            ctx.sVE.inc(v.tensor_copy(out=dst_bf[:, k, :],
                                      in_=ctx.xin[:, (ring_base + k) % ctx.nring, 0:ncols]))


def emit_tokmaj_to_dmajor(ctx, block, src_ap, dst_bf, tile_idx, src_is_bf=False,
                          evac_extra_waits=()):
    slot = tile_idx % 2
    if not src_is_bf:
        @block.vector
        def _(v):
            wait(v, ctx.sPE)
            ctx.sVE.inc(v.tensor_copy(out=ctx.xbf[:, slot, :], in_=src_ap))
        src = ctx.xbf[:, slot, :]
    else:
        src = src_ap

    @block.tensor
    def _(t):
        wait(t, ctx.sVE)
        wait(t, ctx.sIN, ctx.const_ready_in)
        for j in range(8):
            i = t.transpose(out=ctx.p6[:, j * 128:(j + 1) * 128],
                            in_=src[:, j * 128:(j + 1) * 128], identity=ctx.idbf[:])
        ctx.sPE.inc(i)
    tdone = ctx.sPE.v

    @block.vector
    def _(v):
        wait(v, ctx.sPE, tdone)
        for sem, val in evac_extra_waits:
            wait(v, sem, val)
        ctx.sVE.inc(v.tensor_copy(
            out=dst_bf[:, 0:8, tile_idx * 128:tile_idx * 128 + 128],
            in_=ctx.p6[:].rearrange("p (j t) -> p j t", j=8)))


def emit_l2norm_proj(ctx, block, wsel, dst_ap, rhs_sel, sq_sb, raw_sb):
    @block.tensor
    def _(t):
        wait(t, ctx.sVE)
        for k in range(8):
            i = t.matmul(out=ctx.p0[:], lhsT=wsel(k), rhs=rhs_sel(k),
                         start=(k == 0), stop=(k == 7))
        ctx.sPE.inc(i)
    mm_done = ctx.sPE.v

    @block.vector
    def _(v):
        wait(v, ctx.sPE, mm_done)
        v.tensor_copy(out=raw_sb[:], in_=ctx.p0[:])
        ctx.sVE.inc(v.tensor_tensor(out=sq_sb[:], in0=raw_sb[:], in1=raw_sb[:],
                                    op=ALU.mult))
    evac_done = ctx.sVE.v

    @block.tensor
    def _(t):
        wait(t, ctx.sVE, evac_done)
        wait(t, ctx.sIN, ctx.const_ready_in)
        ctx.sPE.inc(t.matmul(out=ctx.p3[0:2, :], lhsT=ctx.hsel[:], rhs=sq_sb[:],
                             start=True, stop=True))
    ss_done = ctx.sPE.v

    @block.vector
    def _(v):
        wait(v, ctx.sPE, ss_done)
        ctx.sVE.inc(v.tensor_scalar(out=ctx.ss_sb[0:2, :], in0=ctx.p3[0:2, :],
                                    scalar1=6e-20, scalar2=None, op0=ALU.max))
    clip_done = ctx.sVE.v

    @block.scalar
    def _(s):
        wait(s, ctx.sVE, clip_done)
        s.activation(out=ctx.rn_sb[0:2, :], in_=ctx.ss_sb[0:2, :], func=AF.Ln)
        ctx.sSC.inc(s.activation(out=ctx.rn_sb[0:2, :], in_=ctx.rn_sb[0:2, :],
                                 func=AF.Exp, scale=-0.5))
    rn_done = ctx.sSC.v

    @block.tensor
    def _(t):
        wait(t, ctx.sSC, rn_done)
        wait(t, ctx.sVE)
        ctx.sPE.inc(t.matmul(out=ctx.p7[:], lhsT=ctx.hselt[:], rhs=ctx.rn_sb[0:2, :],
                             start=True, stop=True))
    bc_done = ctx.sPE.v

    @block.vector
    def _(v):
        wait(v, ctx.sPE, bc_done)
        ctx.sVE.inc(v.tensor_tensor(out=dst_ap, in0=raw_sb[:], in1=ctx.p7[:],
                                    op=ALU.mult))


def emit_ln(ctx, block, y_psum, x_dram, tile_idx, out_dram, slot):
    r = slot % ctx.nres
    zslot = slot % ctx.nz
    oslot = slot % ctx.nout

    @block.sync
    def _(sync):
        wait(sync, ctx.sVE)
        ctx.sIN.inc(sync.dma_start(
            out=ctx.xres[:, r, :],
            in_=x_dram[tile_idx * 128:(tile_idx + 1) * 128, :]), 16)
    res_done = ctx.sIN.v

    @block.vector
    def _(v):
        wait(v, ctx.sIN, res_done)
        wait(v, ctx.sPE)
        wait(v, ctx.sSC)
        v.tensor_tensor(out=ctx.z_sb[:, zslot, :], in0=y_psum,
                        in1=ctx.xres[:, r, :], op=ALU.add)
        z = ctx.z_sb[:, zslot, :]
        v.tensor_scalar(out=ctx.zsq[:], in0=z, scalar1=1.0 / DM, scalar2=0.0,
                        op0=ALU.mult, op1=ALU.add, accum_out=ctx.lnst[:, 0:1])
        ctx.sVE.inc(v.scalar_tensor_tensor(out=ctx.zsq[:], in0=z, scalar=1.0 / DM,
                                           in1=z, op0=ALU.mult, op1=ALU.mult,
                                           accum_out=ctx.lnst[:, 1:2]))
    z_done = ctx.sVE.v

    @block.gpsimd
    def _(g):
        wait(g, ctx.sVE, z_done)
        g.tensor_scalar(out=ctx.lnst[:, 2:3], in0=ctx.lnst[:, 0:1],
                        scalar1=-1.0, scalar2=None, op0=ALU.mult)
        ctx.sGP.inc(g.tensor_tensor(out=ctx.lnst[:, 3:4], in0=ctx.lnst[:, 0:1],
                                    in1=ctx.lnst[:, 0:1], op=ALU.mult))
    g_done = ctx.sGP.v

    @block.vector
    def _(v):
        wait(v, ctx.sGP, g_done)
        ctx.sVE.inc(v.tensor_tensor(out=ctx.lnst[:, 4:5], in0=ctx.lnst[:, 1:2],
                                    in1=ctx.lnst[:, 3:4], op=ALU.subtract))
    var_done = ctx.sVE.v

    @block.scalar
    def _(s):
        wait(s, ctx.sVE, var_done)
        wait(s, ctx.sST)
        s.activation(out=ctx.lnst[:, 6:7], in_=ctx.lnst[:, 4:5], func=AF.Ln,
                     bias=ctx.ceps[:])
        s.activation(out=ctx.lnst[:, 6:7], in_=ctx.lnst[:, 6:7],
                     func=AF.Exp, scale=-0.5)
        s.activation(out=ctx.x1o[:, oslot, :], in_=ctx.z_sb[:, zslot, :],
                     func=AF.Identity, bias=ctx.lnst[:, 2:3])
        ctx.sSC.inc(s.activation(out=ctx.x1o[:, oslot, :], in_=ctx.x1o[:, oslot, :],
                                 func=AF.Identity, scale=ctx.lnst[:, 6:7]))
    app_done = ctx.sSC.v

    @block.scalar
    def _(s):
        wait(s, ctx.sSC, app_done)
        ctx.sST.inc(s.dma_start(
            out=out_dram[tile_idx * 128:(tile_idx + 1) * 128, :],
            in_=ctx.x1o[:, oslot, :]), 16)


def emit_final_wait(ctx, block):
    @block.scalar
    def _(s):
        wait(s, ctx.sST)


def build_p0(nc, x0_d, wq_d, wkv_d, wo_d, x1_d, dbg=None, pfx="", cdram=None):
    with contextlib.ExitStack() as st:
        ctx = Ctx()
        common_setup(ctx, st, nc, slim=True, pfx=pfx, cdram=cdram)
        wq_bf = st.enter_context(nc.sbuf_tensor(pfx + "wq_bf", [128, 8, 512], BF16))
        wkv_bf = st.enter_context(nc.sbuf_tensor(pfx + "wkv_bf", [128, 8, 1024], BF16))
        wo_bf = st.enter_context(nc.sbuf_tensor(pfx + "wo_bf", [128, 4, 1024], BF16))
        x0t = st.enter_context(nc.sbuf_tensor(pfx + "x0t", [128, 8, 512], BF16))
        kt_ = st.enter_context(nc.sbuf_tensor(pfx + "kt", [128, 4, SEQ], BF16))
        qt = st.enter_context(nc.sbuf_tensor(pfx + "qt", [128, 4, SEQ], BF16))
        v_sb = st.enter_context(nc.sbuf_tensor(pfx + "v_sb", [128, NT, 520], BF16))
        ctxT = st.enter_context(nc.sbuf_tensor(pfx + "ctxT", [128, 4, 512], BF16))
        ptP = st.enter_context(nc.sbuf_tensor(pfx + "ptP", [128, 3, 1024], BF16))
        ptA = ptP[:, :, 0:512]
        ptB = ptP[:, :, 512:1024]
        raw = st.enter_context(nc.sbuf_tensor(pfx + "raw", [128, 2, 512], F32))
        sq_sb = st.enter_context(nc.sbuf_tensor(pfx + "sq_sb", [128, 512], F32))
        ctx.ss_sb = st.enter_context(nc.sbuf_tensor(pfx + "ss_sb", [2, 512], F32))
        ctx.rn_sb = st.enter_context(nc.sbuf_tensor(pfx + "rn_sb", [2, 512], F32))
        scA = st.enter_context(nc.sbuf_tensor(pfx + "scA", [1, 512], F32))
        scB = st.enter_context(nc.sbuf_tensor(pfx + "scB", [1, 512], F32))
        rsA = st.enter_context(nc.sbuf_tensor(pfx + "rsA", [1, 512], F32))
        rsB = st.enter_context(nc.sbuf_tensor(pfx + "rsB", [1, 512], F32))
        bcvA = st.enter_context(nc.sbuf_tensor(pfx + "bcvA", [64, 512], F32))
        bcvB = st.enter_context(nc.sbuf_tensor(pfx + "bcvB", [64, 512], F32))

        block = st.enter_context(nc.Block())
        emit_const_loads(ctx, block)
        emit_weight_load(ctx, block, wq_bf, wq_d, 512, 8)
        emit_weight_load(ctx, block, wkv_bf, wkv_d, 1024, 8, ring_base=8)
        emit_weight_load(ctx, block, wo_bf, wo_d, 1024, 4, ring_base=16)

        @block.vector
        def _(v):
            for h in range(8):
                i = v.memset(v_sb[:, :, h * 65 + 64:h * 65 + 65], 1.0)
            ctx.sVE.inc(i)

        for c in range(NCH):
            for tt in range(4):
                g = c * 4 + tt
                @block.sync
                def _(sync):
                    wait(sync, ctx.sVE)
                    ctx.sIN.inc(sync.dma_start(
                        out=ctx.xin[:, g % ctx.nring, :],
                        in_=x0_d[g * 128:(g + 1) * 128, :]), 16)
                ld = ctx.sIN.v

                @block.vector
                def _(v):
                    wait(v, ctx.sIN, ld)
                emit_tokmaj_to_dmajor(ctx, block, ctx.xin[:, g % ctx.nring, :], x0t, tt)

            chains = []
            for wmat, dst in ((wkv_bf, kt_), (wq_bf, qt)):
                for j in range(4):
                    chains.append((wmat, j, dst[:, j, c * 512:(c + 1) * 512]))
            mm_te = {}
            sq_ve = {}
            ss_te = {}
            clip_ve = {}
            rn_sc = {}
            bc_te = {}
            apply_ve = {}

            def emit_bc_apply(i):
                @block.tensor
                def _(t):
                    wait(t, ctx.sSC, rn_sc[i])
                    if i >= 1:
                        wait(t, ctx.sVE, apply_ve[i - 1])
                    ctx.sPE.inc(t.matmul(out=ctx.p7[:], lhsT=ctx.hselt[:],
                                         rhs=ctx.rn_sb[0:2, :], start=True,
                                         stop=True, skip_group_check=True))
                bc_te[i] = ctx.sPE.v

                @block.vector
                def _(v):
                    wait(v, ctx.sPE, bc_te[i])
                    ctx.sVE.inc(v.tensor_tensor(out=chains[i][2],
                                                in0=raw[:, i % 2, :], in1=ctx.p7[:],
                                                op=ALU.mult))
                apply_ve[i] = ctx.sVE.v

            for i, (wmat, j, dst_ap) in enumerate(chains):
                @block.tensor
                def _(t):
                    if i >= 1:
                        wait(t, ctx.sVE, sq_ve[i - 1])
                    else:
                        wait(t, ctx.sVE)
                    for k in range(8):
                        mmi = t.matmul(out=ctx.p0[:], lhsT=wmat[:, k, j * 128:(j + 1) * 128],
                                       rhs=x0t[:, k, :], start=(k == 0), stop=(k == 7),
                                       skip_group_check=True)
                    ctx.sPE.inc(mmi)
                mm_te[i] = ctx.sPE.v
                if i >= 1:
                    emit_bc_apply(i - 1)

                @block.vector
                def _(v):
                    wait(v, ctx.sPE, mm_te[i])
                    if i >= 1:
                        wait(v, ctx.sPE, ss_te[i - 1])
                    v.tensor_copy(out=raw[:, i % 2, :], in_=ctx.p0[:])
                    ctx.sVE.inc(v.tensor_tensor(out=sq_sb[:], in0=raw[:, i % 2, :],
                                                in1=raw[:, i % 2, :], op=ALU.mult))
                sq_ve[i] = ctx.sVE.v

                @block.tensor
                def _(t):
                    wait(t, ctx.sVE, sq_ve[i])
                    wait(t, ctx.sIN, ctx.const_ready_in)
                    if i >= 1:
                        wait(t, ctx.sVE, clip_ve[i - 1])
                    ctx.sPE.inc(t.matmul(out=ctx.p3[0:2, :], lhsT=ctx.hsel[:],
                                         rhs=sq_sb[:], start=True, stop=True,
                                         skip_group_check=True))
                ss_te[i] = ctx.sPE.v

                @block.vector
                def _(v):
                    wait(v, ctx.sPE, ss_te[i])
                    if i >= 1:
                        wait(v, ctx.sSC, rn_sc[i - 1])
                    ctx.sVE.inc(v.tensor_scalar(out=ctx.ss_sb[0:2, :], in0=ctx.p3[0:2, :],
                                                scalar1=6e-20, scalar2=None, op0=ALU.max))
                clip_ve[i] = ctx.sVE.v

                @block.scalar
                def _(s):
                    wait(s, ctx.sVE, clip_ve[i])
                    if i >= 1:
                        wait(s, ctx.sPE, bc_te[i - 1])
                    s.activation(out=ctx.rn_sb[0:2, :], in_=ctx.ss_sb[0:2, :], func=AF.Ln)
                    ctx.sSC.inc(s.activation(out=ctx.rn_sb[0:2, :], in_=ctx.rn_sb[0:2, :],
                                             func=AF.Exp, scale=-0.5))
                rn_sc[i] = ctx.sSC.v
            emit_bc_apply(7)

            for tt in range(4):
                g = c * 4 + tt
                @block.tensor
                def _(t):
                    wait(t, ctx.sVE)
                    for k in range(8):
                        i = t.matmul(out=ctx.p45[:, 0:512],
                                     lhsT=x0t[:, k, tt * 128:(tt + 1) * 128],
                                     rhs=wkv_bf[:, k, 512:1024],
                                     start=(k == 0), stop=(k == 7))
                    ctx.sPE.inc(i)
                vm = ctx.sPE.v

                @block.vector
                def _(v):
                    wait(v, ctx.sPE, vm)
                    ctx.sVE.inc(v.tensor_copy(
                        out=v_sb[:, g, 0:520].rearrange("p (h d) -> p h d", d=65)[:, :, 0:64],
                        in_=ctx.p45[:, 0:512].rearrange("p (h d) -> p h d", d=64)))

        prev_exp_last = 0
        prev_bcv = 0
        lnz_guard = 0
        prev_bcv_ve = 0
        prev_exp_even = 0
        for c in range(NCH):
            for jo in range(4):
                hA, hB = 2 * jo, 2 * jo + 1
                cnt = ctx.sSC.v
                expA = []
                expB = []
                for _kt in range(NT):
                    if _kt % 2 == 0:
                        expA.append(cnt + 1)
                        expB.append(cnt + 2)
                        cnt += 2
                    else:
                        cnt += 1
                        expA.append(cnt)
                        expB.append(cnt)
                rsA_proj = cnt + 1
                rsB_proj = cnt + 2
                sVE_all = ctx.sVE.v
                qk_act = []
                pv_act = []

                @block.tensor
                def _(t):
                    def emit_pv(pv, waited):
                        if pv == 0:
                            wait(t, ctx.sVE, sVE_all)
                        if not waited:
                            wait_f(t, ctx.sSC, expB[pv])
                        t.matmul(out=ctx.p2[0:65, :],
                                 lhsT=v_sb[:, pv, hA * 65:hA * 65 + 65],
                                 rhs=ptA[:, pv % 3, :],
                                 start=(pv == 0), stop=(pv == NT - 1),
                                 skip_group_check=True)
                        ctx.sPE.inc(t.matmul(
                            out=ctx.p3[0:65, :],
                            lhsT=v_sb[:, pv, hB * 65:hB * 65 + 65],
                            rhs=ptB[:, pv % 3, :],
                            start=(pv == 0), stop=(pv == NT - 1),
                            skip_group_check=True))
                        pv_act.append(ctx.sPE.v)
                    for kt in range(NT):
                        if kt == 0:
                            if prev_exp_even:
                                wait(t, ctx.sSC, prev_exp_even)
                            if c == 0 and jo == 0:
                                wait(t, ctx.sSC)
                                wait(t, ctx.sVE, sVE_all)
                        if kt == 1:
                            if prev_exp_last:
                                wait(t, ctx.sSC, prev_exp_last)
                            if jo == 0:
                                wait(t, ctx.sVE, lnz_guard)
                            if prev_bcv_ve:
                                wait(t, ctx.sVE, prev_bcv_ve)
                        if kt >= 2:
                            wait_f(t, ctx.sSC, expB[kt - 2])
                        if kt % 2 == 0:
                            plA, plB = ctx.p0[:], ctx.p1[:]
                        else:
                            plA, plB = ctx.p45[:, 0:512], ctx.p45[:, 512:1024]
                        t.matmul(out=plA,
                                 lhsT=kt_[0:64, jo, kt * 128:(kt + 1) * 128],
                                 rhs=qt[0:64, jo, c * 512:(c + 1) * 512],
                                 start=True, stop=True, skip_group_check=True)
                        ctx.sPE.inc(t.matmul(
                            out=plB,
                            lhsT=kt_[64:128, jo, kt * 128:(kt + 1) * 128],
                            rhs=qt[64:128, jo, c * 512:(c + 1) * 512],
                            start=True, stop=True, skip_group_check=True))
                        qk_act.append(ctx.sPE.v)
                        if kt >= 2:
                            emit_pv(kt - 2, waited=True)
                    emit_pv(NT - 2, waited=False)
                    emit_pv(NT - 1, waited=False)
                    if prev_bcv_ve:
                        wait(t, ctx.sVE, prev_bcv_ve)
                    wait_f(t, ctx.sSC, rsA_proj)
                    t.matmul(out=ctx.p7[0:64, :], lhsT=ctx.ones64[:],
                             rhs=rsA[:], start=True, stop=True,
                             skip_group_check=True)
                    wait_f(t, ctx.sSC, rsB_proj)
                    ctx.sPE.inc(t.matmul(out=ctx.p45[0:64, 0:512], lhsT=ctx.ones64[:],
                                         rhs=rsB[:], start=True, stop=True,
                                         skip_group_check=True))
                bc_act = ctx.sPE.v

                @block.vector
                def _(v):
                    wait(v, ctx.sPE, pv_act[NT - 1])
                    v.tensor_scalar(out=scA[:], in0=ctx.p2[64:65, :],
                                    scalar1=1e-6, scalar2=None, op0=ALU.max)
                    ctx.sVE.inc(v.tensor_scalar(out=scB[:], in0=ctx.p3[64:65, :],
                                                scalar1=1e-6, scalar2=None, op0=ALU.max))
                    clip_act = ctx.sVE.v
                    wait(v, ctx.sPE, bc_act)
                    v.tensor_copy(out=bcvA[:], in_=ctx.p7[0:64, :])
                    v.tensor_copy(out=bcvB[:], in_=ctx.p45[0:64, 0:512])
                    v.tensor_tensor(out=ctxT[0:64, jo, :],
                                    in0=ctx.p2[0:64, :], in1=bcvA[:],
                                    op=ALU.mult)
                    ctx.sVE.inc(v.tensor_tensor(out=ctxT[64:128, jo, :],
                                                in0=ctx.p3[0:64, :], in1=bcvB[:],
                                                op=ALU.mult))
                clip_act = ctx.sVE.v - 1
                prev_bcv_ve = ctx.sVE.v

                @block.scalar
                def _(s):
                    for kt in range(NT):
                        wait(s, ctx.sPE, qk_act[kt])
                        if kt >= 3:
                            wait(s, ctx.sPE, pv_act[kt - 3])
                        if kt % 2 == 0:
                            plA, plB = ctx.p0[:], ctx.p1[:]
                        else:
                            plA, plB = ctx.p45[:, 0:512], ctx.p45[:, 512:1024]
                        if kt % 2 == 1:
                            ctx.sSC.inc(s.activation(
                                out=ptP[:, kt % 3, :], in_=ctx.p45[:, 0:1024],
                                func=AF.Exp, scale=16.0, bias=ctx.cneg16[:]))
                        else:
                            ctx.sSC.inc(s.activation(out=ptA[:, kt % 3, :], in_=plA,
                                                     func=AF.Exp, scale=16.0,
                                                     bias=ctx.cneg16[:]))
                            ctx.sSC.inc(s.activation(out=ptB[:, kt % 3, :], in_=plB,
                                                     func=AF.Exp, scale=16.0,
                                                     bias=ctx.cneg16[:]))
                    wait(s, ctx.sVE, clip_act)
                    s.activation(out=rsA[:], in_=scA[:], func=AF.Ln)
                    ctx.sSC.inc(s.activation(out=rsA[:], in_=rsA[:],
                                             func=AF.Exp, scale=-1.0))
                    s.activation(out=rsB[:], in_=scB[:], func=AF.Ln)
                    ctx.sSC.inc(s.activation(out=rsB[:], in_=rsB[:],
                                             func=AF.Exp, scale=-1.0))
                assert ctx.sSC.v == rsB_proj, (ctx.sSC.v, rsB_proj)
                prev_exp_last = expB[NT - 1]
                prev_exp_even = expB[NT - 2]

            for tt in range(4):
                g = c * 4 + tt
                @block.tensor
                def _(t):
                    wait(t, ctx.sVE)
                    for half in range(2):
                        for k in range(4):
                            i = t.matmul(out=ctx.p45[:, half * 512:(half + 1) * 512],
                                         lhsT=ctxT[:, k, tt * 128:(tt + 1) * 128],
                                         rhs=wo_bf[:, k, half * 512:(half + 1) * 512],
                                         start=(k == 0), stop=(k == 3))
                    ctx.sPE.inc(i)
                emit_ln(ctx, block, ctx.p45[:], x0_d, g, x1_d, g)
            lnz_guard = ctx.sVE.v

        emit_final_wait(ctx, block)
    return nc


def build_p1(nc, x1_d, bw_d, bb_d, la_d, lb_d, rw_d, rb_d, x2_d, dbg=None, pfx="", cdram=None):
    with contextlib.ExitStack() as st:
        ctx = Ctx()
        common_setup(ctx, st, nc, pfx=pfx, cdram=cdram)
        bwT_bf = st.enter_context(nc.sbuf_tensor(pfx + "bwT_bf", [128, 8, 1024], BF16))
        la_bf = st.enter_context(nc.sbuf_tensor(pfx + "la_bf", [128, 8, 8], BF16))
        rwT_bf = st.enter_context(nc.sbuf_tensor(pfx + "rwT_bf", [128, 8, 8], BF16))
        b2_bf = st.enter_context(nc.sbuf_tensor(pfx + "b2_bf", [64, 1024], BF16))
        bb_sb = st.enter_context(nc.sbuf_tensor(pfx + "bb_sb", [1, 1024], F32))
        rb_sb = st.enter_context(nc.sbuf_tensor(pfx + "rb_sb", [1, 8], F32))
        rw_sb = st.enter_context(nc.sbuf_tensor(pfx + "rw_sb", [8, 1024], F32))
        rwbf_sb = st.enter_context(nc.sbuf_tensor(pfx + "rwbf_sb", [8, 1024], BF16))
        b2f_sb = st.enter_context(nc.sbuf_tensor(pfx + "b2f_sb", [64, 1024], F32))
        x1t = st.enter_context(nc.sbuf_tensor(pfx + "x1t", [128, 8, 128], BF16))
        eg_sb = st.enter_context(nc.sbuf_tensor(pfx + "eg_sb", [128, 8], F32))
        g_bf = st.enter_context(nc.sbuf_tensor(pfx + "g_bf", [128, 8], BF16))
        g_st = st.enter_context(nc.sbuf_tensor(pfx + "g_st", [128, 4], F32))
        c_bf = st.enter_context(nc.sbuf_tensor(pfx + "c_bf", [128, 64], BF16))
        ct_sb = st.enter_context(nc.sbuf_tensor(pfx + "ct_sb", [64, 128], BF16))

        block = st.enter_context(nc.Block())
        emit_const_loads(ctx, block)
        for r in range(8):
            @block.sync
            def _(sync):
                wait(sync, ctx.sVE)
                ctx.sIN.inc(sync.dma_start(out=ctx.xin[:, r % ctx.nring, :],
                                           in_=bw_d[r * 128:(r + 1) * 128, :]), 16)
            ld = ctx.sIN.v

            @block.vector
            def _(v):
                wait(v, ctx.sIN, ld)
            emit_tokmaj_to_dmajor(ctx, block, ctx.xin[:, r % ctx.nring, :], bwT_bf, r)

        @block.sync
        def _(sync):
            ctx.sIN.inc(sync.dma_start(out=bb_sb[:], in_=bb_d.unsqueeze(0)), 16)
            ctx.sIN.inc(sync.dma_start(out=rb_sb[:], in_=rb_d.unsqueeze(0)), 16)
            ctx.sIN.inc(sync.dma_start(out=rw_sb[:], in_=rw_d), 16)
            ctx.sIN.inc(sync.dma_start(out=b2f_sb[:], in_=lb_d.rearrange("e r d -> (e r) d")), 16)
            ctx.sIN.inc(sync.dma_start(out=ctx.xin[:, 0, 0:8].rearrange("p a -> p a"),
                                       in_=la_d.rearrange("(k p) r -> p k r", p=128)
                                       if False else la_d[0:128, :]), 16)
        @block.sync
        def _(sync):
            wait(sync, ctx.sVE)
            ctx.sIN.inc(sync.dma_start(
                out=ctx.xin[:, 1, 0:64].rearrange("p (k r) -> p k r", k=8),
                in_=la_d.rearrange("(k p) r -> p k r", p=128)), 16)
        la_ld = ctx.sIN.v
        ctx.smallw_ld = la_ld

        @block.vector
        def _(v):
            wait(v, ctx.sIN, la_ld)
            v.tensor_copy(out=la_bf[:], in_=ctx.xin[:, 1, 0:64].rearrange("p (k r) -> p k r", k=8))
            v.tensor_copy(out=b2_bf[:], in_=b2f_sb[:])
            ctx.sVE.inc(v.tensor_copy(out=rwbf_sb[:], in_=rw_sb[:]))
        smallw_done = ctx.sVE.v

        for j in range(8):
            @block.tensor
            def _(t):
                wait(t, ctx.sVE, smallw_done)
                wait(t, ctx.sVE)
                wait(t, ctx.sIN, ctx.const_ready_in)
                ctx.sPE.inc(t.transpose(out=ctx.p6[:, 0:8], in_=rwbf_sb[0:8, j * 128:(j + 1) * 128],
                                        identity=ctx.idbf[0:8, 0:8]))
            td = ctx.sPE.v

            @block.vector
            def _(v):
                wait(v, ctx.sPE, td)
                ctx.sVE.inc(v.tensor_copy(out=rwT_bf[:, j, :], in_=ctx.p6[:, 0:8]))

        for g in range(NT):
            @block.sync
            def _(sync):
                wait(sync, ctx.sVE)
                ctx.sIN.inc(sync.dma_start(out=ctx.xin[:, g % ctx.nring, :],
                                           in_=x1_d[g * 128:(g + 1) * 128, :]), 16)
            ld = ctx.sIN.v

            @block.vector
            def _(v):
                wait(v, ctx.sIN, ld)
            emit_tokmaj_to_dmajor(ctx, block, ctx.xin[:, g % ctx.nring, :], x1t, 0)
            xt_done = ctx.sVE.v

            @block.tensor
            def _(t):
                wait(t, ctx.sVE, xt_done)
                wait(t, ctx.sIN, ctx.smallw_ld)
                wait(t, ctx.sSC)
                wait(t, ctx.sVE)
                for k in range(8):
                    t.matmul(out=ctx.p0[:, 0:8], lhsT=x1t[:, k, :], rhs=la_bf[:, k, :],
                             start=(k == 0), stop=(k == 7))
                for k in range(8):
                    i = t.matmul(out=ctx.p1[:, 0:8], lhsT=x1t[:, k, :], rhs=rwT_bf[:, k, :],
                                 start=(k == 0), stop=False, skip_group_check=True)
                ctx.sPE.inc(t.matmul(out=ctx.p1[:, 0:8], lhsT=ctx.ones128[:], rhs=rb_sb[:],
                                     start=False, stop=True, skip_group_check=True))
                for half in range(2):
                    for k in range(8):
                        t.matmul(out=ctx.p45[:, half * 512:(half + 1) * 512],
                                 lhsT=x1t[:, k, :],
                                 rhs=bwT_bf[:, k, half * 512:(half + 1) * 512],
                                 start=(k == 0), stop=False, skip_group_check=True)
                    i = t.matmul(out=ctx.p45[:, half * 512:(half + 1) * 512],
                                 lhsT=ctx.ones128[:], rhs=bb_sb[:, half * 512:(half + 1) * 512],
                                 start=False, stop=False, skip_group_check=True)
                ctx.sPE.inc(i)
            midrt_done = ctx.sPE.v - 1

            @block.vector
            def _(v):
                wait(v, ctx.sPE, midrt_done)
                ctx.sVE.inc(v.reduce_max(out=g_st[:, 0:1], in_=ctx.p1[:, 0:8], axis=AX.X))
            mx_done = ctx.sVE.v

            @block.gpsimd
            def _(gp):
                wait(gp, ctx.sVE, mx_done)
                ctx.sGP.inc(gp.tensor_scalar(out=g_st[:, 1:2], in0=g_st[:, 0:1],
                                             scalar1=-1.0, scalar2=None, op0=ALU.mult))
            nmx_done = ctx.sGP.v

            @block.scalar
            def _(s):
                wait(s, ctx.sGP, nmx_done)
                ctx.sSC.inc(s.activation(out=eg_sb[:], in_=ctx.p1[:, 0:8], func=AF.Exp,
                                         bias=g_st[:, 1:2]))
            eg_done = ctx.sSC.v

            @block.vector
            def _(v):
                wait(v, ctx.sSC, eg_done)
                ctx.sVE.inc(v.reduce_sum(out=g_st[:, 2:3], in_=eg_sb[:], axis=AX.X))
            sm_done = ctx.sVE.v

            @block.scalar
            def _(s):
                wait(s, ctx.sVE, sm_done)
                s.activation(out=g_st[:, 3:4], in_=g_st[:, 2:3], func=AF.Ln)
                s.activation(out=g_st[:, 3:4], in_=g_st[:, 3:4], func=AF.Exp, scale=-1.0)
                ctx.sSC.inc(s.activation(out=g_bf[:], in_=eg_sb[:], func=AF.Identity,
                                         scale=g_st[:, 3:4]))
            gbf_done = ctx.sSC.v

            @block.vector
            def _(v):
                wait(v, ctx.sSC, gbf_done)
                ctx.sVE.inc(v.scalar_tensor_tensor(
                    out=c_bf[:].rearrange("p (e r) -> p e r", e=8),
                    in0=ctx.p0[:, 0:8].unsqueeze(1).broadcast_to([128, 8, 8]),
                    scalar=2.0,
                    in1=g_bf[:].unsqueeze(2).broadcast_to([128, 8, 8]),
                    op0=ALU.mult, op1=ALU.mult))
            c_done = ctx.sVE.v

            @block.tensor
            def _(t):
                wait(t, ctx.sVE, c_done)
                ctx.sPE.inc(t.transpose(out=ctx.p6[0:64, 0:128], in_=c_bf[:],
                                        identity=ctx.idbf[:]))
            ct_done = ctx.sPE.v

            @block.vector
            def _(v):
                wait(v, ctx.sPE, ct_done)
                ctx.sVE.inc(v.tensor_copy(out=ct_sb[:], in_=ctx.p6[0:64, 0:128]))
            cte_done = ctx.sVE.v

            @block.tensor
            def _(t):
                wait(t, ctx.sVE, cte_done)
                for half in range(2):
                    ctx.sPE.inc(t.matmul(out=ctx.p45[:, half * 512:(half + 1) * 512],
                                         lhsT=ct_sb[:], rhs=b2_bf[:, half * 512:(half + 1) * 512],
                                         start=False, stop=True, skip_group_check=True))
            emit_ln(ctx, block, ctx.p45[:], x1_d, g, x2_d, g)

        emit_final_wait(ctx, block)
    return nc


def build_p2(nc, x2_d, wqkv_d, wo_d, x3_d, qspill, kspill, pfx="", cdram=None):
    NH = 16
    with contextlib.ExitStack() as st:
        ctx = Ctx()
        common_setup(ctx, st, nc, want=("IDBF", "TRIT"),
                     slim=True, pfx=pfx, cdram=cdram)
        if cdram is not None:
            ropec_d = cdram["ROPE_COS"]
            ropes_d = cdram["ROPE_SIN"]
        else:
            ropec_d = nc.dram_tensor("ROPE_COS", [128, NT, 32], F32, kind="ExternalInput").ap()
            ropes_d = nc.dram_tensor("ROPE_SIN", [128, NT, 32], F32, kind="ExternalInput").ap()
        ropb = st.enter_context(nc.sbuf_tensor(pfx + "ropb", [128, 2, 2, 32], F32))
        big8 = st.enter_context(nc.sbuf_tensor(pfx + "big8", [128, 8, SEQ], BF16))
        wqkv_bf = big8[:, :, 0:3072].rearrange("p k n -> p k n")
        kt_sb = big8
        wo_bf = st.enter_context(nc.sbuf_tensor(pfx + "wo_bf", [128, 8, 1024], BF16))
        v_sb = st.enter_context(nc.sbuf_tensor(pfx + "v_sb", [128, NT, NH * 65], BF16))
        x2t_t = st.enter_context(nc.sbuf_tensor(pfx + "x2t_t", [128, 8, 128], BF16))
        qk_sp = st.enter_context(nc.sbuf_tensor(pfx + "qk_sp", [128, 8, 128], BF16))
        ctxT = st.enter_context(nc.sbuf_tensor(pfx + "ctxT2", [128, 8, 512], BF16))
        ptA = st.enter_context(nc.sbuf_tensor(pfx + "pt2A", [128, 2, 512], BF16))
        ptB = st.enter_context(nc.sbuf_tensor(pfx + "pt2B", [128, 2, 512], BF16))
        qss = st.enter_context(nc.sbuf_tensor(pfx + "qss", [128, NT, NH], BF16))
        kss = st.enter_context(nc.sbuf_tensor(pfx + "kss", [128, NT, NH], BF16))
        mrow = st.enter_context(nc.sbuf_tensor(pfx + "mrow", [1, 32], F32))
        lrow = st.enter_context(nc.sbuf_tensor(pfx + "lrow", [1, 32], F32))
        urow = st.enter_context(nc.sbuf_tensor(pfx + "urow", [1, 16], F32))
        ub_sb = st.enter_context(nc.sbuf_tensor(pfx + "ub_sb", [128, 16], F32))
        mm2 = st.enter_context(nc.sbuf_tensor(pfx + "mm2", [32, 1], BF16))
        stat2 = st.enter_context(nc.sbuf_tensor(pfx + "stat2", [128, 32], BF16))
        scp = st.enter_context(nc.sbuf_tensor(pfx + "scp2", [33, 512], F32))
        scA = scp[0:1, :]
        scB = scp[32:33, :]
        rsA = st.enter_context(nc.sbuf_tensor(pfx + "rsA2", [1, 512], F32))
        rsB = st.enter_context(nc.sbuf_tensor(pfx + "rsB2", [1, 512], F32))
        bcvA = st.enter_context(nc.sbuf_tensor(pfx + "bcvA2", [64, 512], F32))
        bcvB = st.enter_context(nc.sbuf_tensor(pfx + "bcvB2", [64, 512], F32))

        qtc = ctx.xin[:].rearrange("p a b -> p (a b)").bitcast(BF16).rearrange(
            "p (j t) -> p j t", j=8)
        block = st.enter_context(nc.Block())
        emit_const_loads(ctx, block)
        for k in range(8):
            for piece in range(3):
                @block.sync
                def _(sync):
                    wait(sync, ctx.sVE)
                    ctx.sIN.inc(sync.dma_start(
                        out=ctx.xin[:, (k * 3 + piece) % ctx.nring, :],
                        in_=wqkv_d[k * 128:(k + 1) * 128, piece * 1024:(piece + 1) * 1024]), 16)
                ld = ctx.sIN.v

                @block.vector
                def _(v):
                    wait(v, ctx.sIN, ld)
                    ctx.sVE.inc(v.tensor_copy(
                        out=big8[:, k, piece * 1024:(piece + 1) * 1024],
                        in_=ctx.xin[:, (k * 3 + piece) % ctx.nring, :]))
        emit_weight_load(ctx, block, wo_bf, wo_d, 1024, 8, ring_base=0)

        @block.vector
        def _(v):
            for h in range(NH):
                i = v.memset(v_sb[:, :, h * 65 + 64:h * 65 + 65], 1.0)
            ctx.sVE.inc(i)

        for g in range(NT):
            @block.sync
            def _(sync):
                wait(sync, ctx.sVE)
                ctx.sIN.inc(sync.dma_start(out=ctx.xin[:, g % ctx.nring, :],
                                           in_=x2_d[g * 128:(g + 1) * 128, :]), 16)
                wait(sync, ctx.sGP)
                ctx.sIN.inc(sync.dma_start(out=ropb[:, g % 2, 0, :], in_=ropec_d[:, g, :]), 16)
                ctx.sIN.inc(sync.dma_start(out=ropb[:, g % 2, 1, :], in_=ropes_d[:, g, :]), 16)
            ld = ctx.sIN.v

            @block.vector
            def _(v):
                wait(v, ctx.sIN, ld)
            emit_tokmaj_to_dmajor(ctx, block, ctx.xin[:, g % ctx.nring, :], x2t_t, 0)
            xt_done = ctx.sVE.v

            for comp, spill, ss_all in (("q", qspill, qss), ("k", kspill, kss)):
                col0 = 0 if comp == "q" else 1024
                @block.tensor
                def _(t):
                    wait(t, ctx.sVE)
                    wait(t, ctx.sGP)
                    for half in range(2):
                        for k in range(8):
                            i = t.matmul(out=ctx.p45[:, half * 512:(half + 1) * 512],
                                         lhsT=x2t_t[:, k, :],
                                         rhs=wqkv_bf[:, k, col0 + half * 512:col0 + (half + 1) * 512],
                                         start=(k == 0), stop=(k == 7), skip_group_check=True)
                    ctx.sPE.inc(i)
                proj_done = ctx.sPE.v

                q3 = ctx.p45[:].rearrange("p (h two c) -> p h two c", h=NH, two=2)
                cosb = ropb[:, g % 2, 0, :].unsqueeze(1).broadcast_to([128, NH, 32])
                sinb = ropb[:, g % 2, 1, :].unsqueeze(1).broadcast_to([128, NH, 32])
                ra = ctx.z_sb[:, 0, :].rearrange("p (h two c) -> p h two c", h=NH, two=2)
                rb = ctx.z_sb[:, 1, :].rearrange("p (h two c) -> p h two c", h=NH, two=2)
                rope_bf = ctx.xbf[:, 1, :].rearrange("p (h two c) -> p h two c", h=NH, two=2)

                @block.vector
                def _(v):
                    wait(v, ctx.sPE, proj_done)
                    wait(v, ctx.sIN, ld)
                    wait(v, ctx.sGP)
                    v.tensor_tensor(out=ra[:, :, 0, :], in0=q3[:, :, 0, :], in1=cosb, op=ALU.mult)
                    v.scalar_tensor_tensor(out=rb[:, :, 0, :], in0=q3[:, :, 1, :], scalar=-1.0,
                                           in1=sinb, op0=ALU.mult, op1=ALU.mult)
                    v.tensor_tensor(out=ra[:, :, 1, :], in0=q3[:, :, 1, :], in1=cosb, op=ALU.mult)
                    ctx.sVE.inc(v.tensor_tensor(out=rb[:, :, 1, :], in0=q3[:, :, 0, :], in1=sinb,
                                                op=ALU.mult))
                rp_done = ctx.sVE.v

                @block.gpsimd
                def _(gp):
                    wait(gp, ctx.sVE, rp_done)
                    wait(gp, ctx.sSC)
                    wait(gp, ctx.sPE)
                    ctx.sGP.inc(gp.tensor_tensor(out=ctx.xbf[:, 1, :], in0=ctx.z_sb[:, 0, :],
                                                 in1=ctx.z_sb[:, 1, :], op=ALU.add))
                rope_done = ctx.sGP.v

                @block.vector
                def _(v):
                    wait(v, ctx.sGP, rope_done)
                    v.tensor_tensor(out=ctx.zsq[:], in0=ctx.xbf[:, 1, :],
                                    in1=ctx.xbf[:, 1, :], op=ALU.mult)
                    with nc.allow_low_precision(reason="softmax bound only; cancels"):
                        ctx.sVE.inc(v.reduce_sum(
                            out=ss_all[:, g, :],
                            in_=ctx.zsq[:].rearrange("p (h c) -> p h c", h=NH), axis=AX.X))

                emit_tokmaj_to_dmajor(ctx, block, ctx.xbf[:, 1, :], qk_sp, 0, src_is_bf=True,
                                      evac_extra_waits=[(ctx.sST, ctx.sST.v)])
                tr_done = ctx.sVE.v

                @block.scalar
                def _(s):
                    wait(s, ctx.sVE, tr_done)
                    ctx.sST.inc(s.dma_start(out=spill[:, :, g * 128:(g + 1) * 128],
                                            in_=qk_sp[:]), 16)

            @block.tensor
            def _(t):
                wait(t, ctx.sVE)
                wait(t, ctx.sGP)
                for half in range(2):
                    for k in range(8):
                        i = t.matmul(out=ctx.p45[:, half * 512:(half + 1) * 512],
                                     lhsT=x2t_t[:, k, :],
                                     rhs=wqkv_bf[:, k, 2048 + half * 512:2048 + (half + 1) * 512],
                                     start=(k == 0), stop=(k == 7), skip_group_check=True)
                ctx.sPE.inc(i)
            vm = ctx.sPE.v

            @block.vector
            def _(v):
                wait(v, ctx.sPE, vm)
                ctx.sVE.inc(v.tensor_copy(
                    out=v_sb[:, g, :].rearrange("p (h d) -> p h d", d=65)[:, :, 0:64],
                    in_=ctx.p45[:].rearrange("p (h d) -> p h d", d=64)))

        @block.vector
        def _(v):
            wait(v, ctx.sSC)
            wait(v, ctx.sVE)
            with nc.allow_low_precision(reason="softmax bound only; cancels"):
                v.reduce_max(out=stat2[:, 0:16], in_=qss[:].rearrange("p t h -> p h t"), axis=AX.X)
                ctx.sVE.inc(v.reduce_max(out=stat2[:, 16:32],
                                         in_=kss[:].rearrange("p t h -> p h t"), axis=AX.X))
        st_done = ctx.sVE.v

        @block.tensor
        def _(t):
            wait(t, ctx.sVE, st_done)
            ctx.sPE.inc(t.transpose(out=ctx.p6[0:32, 0:128],
                                    in_=stat2[:], identity=ctx.idbf[:]))
        t1_done = ctx.sPE.v

        @block.vector
        def _(v):
            wait(v, ctx.sPE, t1_done)
            with nc.allow_low_precision(reason="softmax bound only; cancels"):
                ctx.sVE.inc(v.reduce_max(out=mm2[0:32, :], in_=ctx.p6[0:32, 0:128], axis=AX.X))
        mm2_done = ctx.sVE.v

        @block.tensor
        def _(t):
            wait(t, ctx.sVE, mm2_done)
            ctx.sPE.inc(t.transpose(out=ctx.p6[0:1, 512:544], in_=mm2[0:32, :],
                                    identity=ctx.idbf[0:32, 0:32]))
        t2_done = ctx.sPE.v

        @block.vector
        def _(v):
            wait(v, ctx.sPE, t2_done)
            ctx.sVE.inc(v.tensor_scalar(out=mrow[:], in0=ctx.p6[0:1, 512:544],
                                        scalar1=6e-20, scalar2=None, op0=ALU.max))
        clip_done = ctx.sVE.v

        @block.scalar
        def _(s):
            wait(s, ctx.sVE, clip_done)
            ctx.sSC.inc(s.activation(out=lrow[:], in_=mrow[:], func=AF.Ln))
        ln_done = ctx.sSC.v

        @block.vector
        def _(v):
            wait(v, ctx.sSC, ln_done)
            ctx.sVE.inc(v.tensor_tensor(out=urow[:], in0=lrow[0:1, 0:16],
                                        in1=lrow[0:1, 16:32], op=ALU.add))
        add_done = ctx.sVE.v

        @block.scalar
        def _(s):
            wait(s, ctx.sVE, add_done)
            s.activation(out=urow[:], in_=urow[:], func=AF.Exp, scale=0.5)
            ctx.sSC.inc(s.activation(out=urow[:], in_=urow[:], func=AF.Identity,
                                     scale=-0.125))
        ub_done = ctx.sSC.v

        @block.tensor
        def _(t):
            wait(t, ctx.sSC, ub_done)
            wait(t, ctx.sVE)
            ctx.sPE.inc(t.matmul(out=ctx.p7[:, 0:16], lhsT=ctx.ones128[:], rhs=urow[:],
                                 start=True, stop=True))
        ubc_done = ctx.sPE.v

        @block.vector
        def _(v):
            wait(v, ctx.sPE, ubc_done)
            ctx.sVE.inc(v.tensor_copy(out=ub_sb[:], in_=ctx.p7[:, 0:16]))
        ub_ready = ctx.sVE.v

        spill_done = ctx.sST.v
        kld = []

        def pl_of(kt):
            return ctx.p0 if kt % 2 == 0 else ctx.p1
        prev_exp_last = 0
        prev_exp_even = 0
        lnz_guard = 0
        prev_bcv_ve = 0
        gp_guard = ctx.sGP.v
        for c in range(NCH):
            @block.sync
            def _(sync):
                wait(sync, ctx.sST, spill_done)
                wait(sync, ctx.sPE)
                ctx.sIN.inc(sync.dma_start(
                    out=kt_sb[:, :, c * 512:(c + 1) * 512],
                    in_=kspill[:, :, c * 512:(c + 1) * 512]), 16)
                ctx.sIN.inc(sync.dma_start(out=qtc[:],
                                           in_=qspill[:, :, c * 512:(c + 1) * 512]), 16)
            kld.append(ctx.sIN.v)

            for jo in range(NH // 2):
                hA, hB = 2 * jo, 2 * jo + 1
                nkt = 4 * c + 4
                diag0 = 4 * c
                sSC_base = ctx.sSC.v
                expA = [sSC_base + 2 * kt + 1 for kt in range(nkt)]
                expB = [sSC_base + 2 * kt + 2 for kt in range(nkt)]
                rsA_proj = sSC_base + 2 * nkt + 1
                rsB_proj = sSC_base + 2 * nkt + 2
                sVE_all = ctx.sVE.v
                mask_projA = {dk: sVE_all + 2 * i + 1 for i, dk in enumerate(range(diag0, nkt))}
                mask_projB = {dk: sVE_all + 2 * i + 2 for i, dk in enumerate(range(diag0, nkt))}
                qk_act = []
                pv_act = []

                def pair_banks(kt):
                    if kt % 2 == 0:
                        return ctx.p0[:], ctx.p1[:]
                    return ctx.p45[:, 0:512], ctx.p45[:, 512:1024]

                @block.tensor
                def _(t):
                    def emit_pv(pv, waited):
                        pvoff = max(0, (pv - diag0) * 128)
                        if pv == 0:
                            wait(t, ctx.sVE, sVE_all)
                        if pv >= diag0:
                            wait_f(t, ctx.sVE, mask_projB[pv])
                        elif not waited:
                            wait_f(t, ctx.sSC, expB[pv])
                        t.matmul(out=ctx.p2[0:65, pvoff:512],
                                 lhsT=v_sb[:, pv, hA * 65:hA * 65 + 65],
                                 rhs=ptA[:, pv % 2, pvoff:512],
                                 start=(pv == 0), stop=(pv == nkt - 1),
                                 skip_group_check=True)
                        ctx.sPE.inc(t.matmul(
                            out=ctx.p3[0:65, pvoff:512],
                            lhsT=v_sb[:, pv, hB * 65:hB * 65 + 65],
                            rhs=ptB[:, pv % 2, pvoff:512],
                            start=(pv == 0), stop=(pv == nkt - 1),
                            skip_group_check=True))
                        pv_act.append(ctx.sPE.v)
                    for kt in range(nkt):
                        off = max(0, (kt - diag0) * 128)
                        if kt == 0:
                            wait(t, ctx.sIN, kld[c])
                            if c == 0 and jo == 0:
                                wait(t, ctx.sVE, sVE_all)
                            if prev_exp_even:
                                wait(t, ctx.sSC, prev_exp_even)
                        if kt == 1:
                            if prev_exp_last:
                                wait(t, ctx.sSC, prev_exp_last)
                            if jo == 0:
                                wait(t, ctx.sVE, lnz_guard)
                            if c == 0 and jo == 0:
                                wait(t, ctx.sGP, gp_guard)
                            if prev_bcv_ve:
                                wait(t, ctx.sVE, prev_bcv_ve)
                        if kt >= 2:
                            wait_f(t, ctx.sSC, expB[kt - 2])
                        plA, plB = pair_banks(kt)
                        t.matmul(out=plA[:, off:512],
                                 lhsT=kt_sb[0:64, jo, kt * 128:(kt + 1) * 128],
                                 rhs=qtc[0:64, jo, off:512],
                                 start=True, stop=True, skip_group_check=True)
                        ctx.sPE.inc(t.matmul(
                            out=plB[:, off:512],
                            lhsT=kt_sb[64:128, jo, kt * 128:(kt + 1) * 128],
                            rhs=qtc[64:128, jo, off:512],
                            start=True, stop=True, skip_group_check=True))
                        qk_act.append(ctx.sPE.v)
                        if kt >= 2:
                            emit_pv(kt - 2, waited=True)
                    emit_pv(nkt - 2, waited=False)
                    emit_pv(nkt - 1, waited=False)
                    if prev_bcv_ve:
                        wait(t, ctx.sVE, prev_bcv_ve)
                    wait_f(t, ctx.sSC, rsA_proj)
                    t.matmul(out=ctx.p7[0:64, :], lhsT=ctx.ones64[:],
                             rhs=rsA[:], start=True, stop=True,
                             skip_group_check=True)
                    wait_f(t, ctx.sSC, rsB_proj)
                    ctx.sPE.inc(t.matmul(out=ctx.p45[0:64, 0:512], lhsT=ctx.ones64[:],
                                         rhs=rsB[:], start=True, stop=True,
                                         skip_group_check=True))
                bc_act = ctx.sPE.v

                @block.vector
                def _(v):
                    wait(v, ctx.sIN, ctx.const_ready_in)
                    for kt in range(diag0, nkt):
                        off = max(0, (kt - diag0) * 128)
                        wait_f(v, ctx.sSC, expA[kt])
                        ctx.sVE.inc(v.tensor_tensor(
                            out=ptA[:, kt % 2, off:off + 128],
                            in0=ptA[:, kt % 2, off:off + 128],
                            in1=ctx.trit[:], op=ALU.mult))
                        assert ctx.sVE.v == mask_projA[kt]
                        wait_f(v, ctx.sSC, expB[kt])
                        ctx.sVE.inc(v.tensor_tensor(
                            out=ptB[:, kt % 2, off:off + 128],
                            in0=ptB[:, kt % 2, off:off + 128],
                            in1=ctx.trit[:], op=ALU.mult))
                        assert ctx.sVE.v == mask_projB[kt]
                    wait(v, ctx.sPE, pv_act[nkt - 1])
                    v.tensor_copy(out=scA, in_=ctx.p2[64:65, :])
                    ctx.sVE.inc(v.tensor_copy(out=scB, in_=ctx.p3[64:65, :]))
                    clip_act2 = ctx.sVE.v
                    wait(v, ctx.sPE, bc_act)
                    v.tensor_copy(out=bcvA[:], in_=ctx.p7[0:64, :])
                    v.tensor_copy(out=bcvB[:], in_=ctx.p45[0:64, 0:512])
                    v.tensor_tensor(out=ctxT[0:64, jo, :],
                                    in0=ctx.p2[0:64, :], in1=bcvA[:],
                                    op=ALU.mult)
                    ctx.sVE.inc(v.tensor_tensor(out=ctxT[64:128, jo, :],
                                                in0=ctx.p3[0:64, :], in1=bcvB[:],
                                                op=ALU.mult))
                clip_act = ctx.sVE.v - 1
                prev_bcv_ve = ctx.sVE.v

                @block.scalar
                def _(s):
                    for kt in range(nkt):
                        off = max(0, (kt - diag0) * 128)
                        wait(s, ctx.sPE, qk_act[kt])
                        if kt >= 2:
                            wait(s, ctx.sPE, pv_act[kt - 2])
                            if kt - 2 >= diag0:
                                wait(s, ctx.sVE, mask_projB[kt - 2])
                        plA, plB = pair_banks(kt)
                        ctx.sSC.inc(s.activation(out=ptA[:, kt % 2, off:512],
                                                 in_=plA[:, off:512],
                                                 func=AF.Exp, scale=0.125,
                                                 bias=ub_sb[:, hA:hA + 1]))
                        ctx.sSC.inc(s.activation(out=ptB[:, kt % 2, off:512],
                                                 in_=plB[:, off:512],
                                                 func=AF.Exp, scale=0.125,
                                                 bias=ub_sb[:, hB:hB + 1]))
                    wait(s, ctx.sVE, clip_act)
                    s.activation(out=rsA[:], in_=scA, func=AF.Ln)
                    ctx.sSC.inc(s.activation(out=rsA[:], in_=rsA[:],
                                             func=AF.Exp, scale=-1.0))
                    s.activation(out=rsB[:], in_=scB, func=AF.Ln)
                    ctx.sSC.inc(s.activation(out=rsB[:], in_=rsB[:],
                                             func=AF.Exp, scale=-1.0))
                assert ctx.sSC.v == rsB_proj
                prev_exp_last = expB[nkt - 1]
                prev_exp_even = expB[nkt - 2]

            for tt in range(4):
                g = c * 4 + tt
                @block.tensor
                def _(t):
                    wait(t, ctx.sVE)
                    for half in range(2):
                        for k in range(8):
                            i = t.matmul(out=ctx.p45[:, half * 512:(half + 1) * 512],
                                         lhsT=ctxT[:, k, tt * 128:(tt + 1) * 128],
                                         rhs=wo_bf[:, k, half * 512:(half + 1) * 512],
                                         start=(k == 0), stop=(k == 7))
                    ctx.sPE.inc(i)
                emit_ln(ctx, block, ctx.p45[:], x2_d, g, x3_d, g)
            lnz_guard = ctx.sVE.v

        emit_final_wait(ctx, block)
    return nc



def build_p3(nc, x3_d, wg_d, wu_d, wd_d, out_d, pfx="", cdram=None, dbg=None):
    IDIM = 2816
    NI = IDIM // 128
    with contextlib.ExitStack() as st:
        ctx = Ctx()
        common_setup(ctx, st, nc, want=("IDBF",), slim=True, nz=1, alloc_zsq=False,
                     alloc_xbf=False, pfx=pfx, cdram=cdram)
        ctx.cln12 = st.enter_context(nc.sbuf_tensor(pfx + "cln12", [128, 1], F32))
        wgT = st.enter_context(nc.sbuf_tensor(pfx + "wgT", [128, 8, IDIM], BF16))
        wuT = st.enter_context(nc.sbuf_tensor(pfx + "wuT", [128, 8, IDIM], BF16))
        wdT = st.enter_context(nc.sbuf_tensor(pfx + "wdT", [128, NI, 1024], BF16))
        qb = {}
        for nm, dt, w in (("A", F32, IDIM), ("Y", F32, IDIM), ("Z", F32, IDIM),
                          ("M", F32, IDIM // 16),
                          ("MC", F32, IDIM // 16), ("R12", F32, IDIM // 16),
                          ("HS", F32, IDIM // 16)):
            qb[nm] = st.enter_context(nc.sbuf_tensor(pfx + "q_" + nm, [128, w], dt))
        sgn = {"A": st.enter_context(nc.sbuf_tensor(pfx + "sgnA", [128, 1408], BF16)),
               "B": st.enter_context(nc.sbuf_tensor(pfx + "sgnB", [128, 1408], BF16))}
        r12d = {"A": st.enter_context(nc.sbuf_tensor(pfx + "r12A", [128, 88], F32)),
                "B": st.enter_context(nc.sbuf_tensor(pfx + "r12B", [128, 88], F32))}
        ctx.zsq = qb["Y"][:, 0:1024]

        block = st.enter_context(nc.Block())
        emit_const_loads(ctx, block)

        @block.vector
        def _(v):
            ctx.sVE.inc(v.memset(ctx.cln12[:], float(np.log(12.0))))

        S = {}
        src_of = {}

        def quant_p1(src_ap, L, qoff, soff, key, sg, src_wait=None):
            nb = L // 16
            SGb = sgn[sg][:, 0:L]
            M = qb["M"][:, soff:soff + nb]
            MC = qb["MC"][:, soff:soff + nb]
            R12 = r12d[sg][:, 0:nb]

            @block.vector
            def _(v):
                if src_wait is not None:
                    wait(v, src_wait[0], src_wait[1])
                wait(v, ctx.sSC, S.get((key, 'r12_sc'), 0))
                v.reduce_max(
                    out=M, in_=src_ap.rearrange("p (b w) -> p b w", w=16),
                    axis=AX.X, apply_absolute_value=True)
                v.scalar_tensor_tensor(out=qb["A"][:, qoff:qoff + L], in0=src_ap,
                                       scalar=-1.0, in1=src_ap,
                                       op0=ALU.mult, op1=ALU.max)
                ctx.sVE.inc(v.tensor_scalar(out=MC, in0=M, scalar1=6e-12,
                                            scalar2=None, op0=ALU.max))
            rm_ve = ctx.sVE.v
            S[(key, 'rm_ve')] = rm_ve
            src_of[key] = src_ap

            @block.scalar
            def _(s):
                if src_wait is not None:
                    wait(s, src_wait[0], src_wait[1])
                wait(s, ctx.sVE, S.get(('sgn', sg), 0))
                s.activation(out=SGb, in_=src_ap, func=AF.Sign)
                wait(s, ctx.sVE, rm_ve)
                s.activation(out=R12, in_=MC, func=AF.Ln)
                ctx.sSC.inc(s.activation(out=R12, in_=R12, func=AF.Exp,
                                         scale=-1.0, bias=ctx.cln12[:]))
            S[(key, 'r12_sc')] = ctx.sSC.v

        def quant_p2(out_ap, L, qoff, soff, key, sg, out_war_te=0):
            nb = L // 16
            A = qb["A"][:, qoff:qoff + L]
            Y = qb["Y"][:, qoff:qoff + L]
            Z = qb["Z"][:, qoff:qoff + L]
            SGb = sgn[sg][:, 0:L]
            MC = qb["MC"][:, soff:soff + nb]
            R12 = r12d[sg][:, 0:nb]
            HS = qb["HS"][:, soff:soff + nb]
            r12_sc = S[(key, 'r12_sc')]

            @block.vector
            def _(v):
                wait(v, ctx.sSC, r12_sc)
                v.tensor_scalar(out=HS, in0=MC, scalar1=1.0 / 12.0,
                                scalar2=None, op0=ALU.mult)
                v.tensor_tensor(
                    out=Y.rearrange("p (b w) -> p b w", w=16),
                    in0=A.rearrange("p (b w) -> p b w", w=16),
                    in1=R12.unsqueeze(2).broadcast_to([128, nb, 16]),
                    op=ALU.mult)
                y, z, a = Y, Z, A
                v.tensor_scalar(out=z, in0=y, scalar1=0.5, scalar2=None, op0=ALU.is_gt)
                v.scalar_tensor_tensor(out=z, in0=y, scalar=1.5, in1=z, op0=ALU.is_gt, op1=ALU.add)
                v.scalar_tensor_tensor(out=z, in0=y, scalar=2.5, in1=z, op0=ALU.is_gt, op1=ALU.add)
                v.scalar_tensor_tensor(out=z, in0=y, scalar=3.5, in1=z, op0=ALU.is_gt, op1=ALU.add)
                v.tensor_scalar(out=a, in0=y, scalar1=5.0, scalar2=None, op0=ALU.is_gt)
                v.scalar_tensor_tensor(out=a, in0=y, scalar=7.0, in1=a, op0=ALU.is_gt, op1=ALU.add)
                v.scalar_tensor_tensor(out=a, in0=y, scalar=10.0, in1=a, op0=ALU.is_gt, op1=ALU.add)
                v.scalar_tensor_tensor(out=a, in0=y, scalar=10.0, in1=a, op0=ALU.is_gt, op1=ALU.add)
                v.scalar_tensor_tensor(out=z, in0=a, scalar=2.0, in1=z, op0=ALU.mult, op1=ALU.add)
                v.tensor_tensor(out=z, in0=z, in1=SGb, op=ALU.mult)
                if out_war_te:
                    wait(v, ctx.sPE, out_war_te)
                ctx.sVE.inc(v.tensor_tensor(
                    out=out_ap.rearrange("p (b w) -> p b w", w=16),
                    in0=Z.rearrange("p (b w) -> p b w", w=16),
                    in1=HS.unsqueeze(2).broadcast_to([128, nb, 16]), op=ALU.mult))
            S[(key, 'ladder_ve')] = ctx.sVE.v
            S[('sgn', sg)] = ctx.sVE.v

        def wtr_evac(src_bf, j0, jn, dst_ap, lad_ve, tr_war_ve):
            @block.tensor
            def _(t):
                wait(t, ctx.sVE, lad_ve)
                wait(t, ctx.sVE, tr_war_ve)
                wait(t, ctx.sIN, ctx.const_ready_in)
                for j in range(jn):
                    i = t.transpose(out=ctx.p6[:, j * 128:(j + 1) * 128],
                                    in_=src_bf[:, (j0 + j) * 128:(j0 + j + 1) * 128],
                                    identity=ctx.idbf[:])
                ctx.sPE.inc(i)
            tr_te = ctx.sPE.v

            @block.vector
            def _(v):
                wait(v, ctx.sPE, tr_te)
                ctx.sVE.inc(v.tensor_copy(
                    out=dst_ap,
                    in_=ctx.p6[:, 0:jn * 128].rearrange("p (j t) -> p j t", j=jn)))
            S[('wevac_ve',)] = ctx.sVE.v
            return tr_te

        wq_cm = nc.sbuf_tensor(pfx + "wq_bf", [128, IDIM], BF16)
        wq_bf = wq_cm.__enter__()
        pend = None
        def flush_w():
            nonlocal pend
            if pend is None:
                return
            slot, qoff, dstT, r = pend
            quant_p2(wq_bf[:, qoff:qoff + 1024], 1024, qoff, slot * 64,
                     ('w', slot), 'A' if slot == 0 else 'B',
                     out_war_te=S.get(('wtr_te', slot), 0))
            S[('xin', slot, 've')] = S[(('w', slot), 'ladder_ve')]
            S[('wtr_te', slot)] = wtr_evac(
                wq_bf[:, qoff:qoff + 1024], 0, 8,
                dstT[:, 0:8, r * 128:r * 128 + 128],
                S[(('w', slot), 'ladder_ve')], S.get(('wevac_ve',), 0))
            pend = None

        wchains = [(w_d, dstT, r) for w_d, dstT in ((wg_d, wgT), (wu_d, wuT))
                   for r in range(NI)]
        wlds = {}

        def emit_wdma(ci):
            w_d, dstT, r = wchains[ci]
            slot = ci % 2

            @block.sync
            def _(sync):
                wait(sync, ctx.sVE, S.get(('xin', slot, 've'), 0))
                wait(sync, ctx.sSC, S.get(('xin', slot, 'sc'), 0))
                ctx.sIN.inc(sync.dma_start(out=ctx.xin[:, slot, :],
                                           in_=w_d[r * 128:(r + 1) * 128, :]), 16)
            wlds[ci] = ctx.sIN.v

        emit_wdma(0)
        for ci, (w_d, dstT, r) in enumerate(wchains):
            slot = ci % 2
            qoff = slot * 1024
            flush_w()
            if ci + 1 < len(wchains):
                emit_wdma(ci + 1)
            quant_p1(ctx.xin[:, slot, :], 1024, qoff, slot * 64,
                     ('w', slot), 'A' if slot == 0 else 'B',
                     src_wait=(ctx.sIN, wlds[ci]))
            S[('xin', slot, 'sc')] = S[(('w', slot), 'r12_sc')]
            pend = (slot, qoff, dstT, r)
        flush_w()
        wgwu_bar = S[('wevac_ve',)]
        for sl in (0, 1):
            S[(('wd', sl), 'ladder_ve')] = ctx.sVE.v
            S[(('wd', sl), 'rm_ve')] = ctx.sVE.v
            S[(('wd', sl), 'r12_sc')] = ctx.sSC.v

        HW = 1408
        pendd = None
        def flush_wd():
            nonlocal pendd
            if pendd is None:
                return
            slot, qoff, r, hh = pendd
            quant_p2(wq_bf[:, qoff:qoff + HW], HW, qoff, slot * 88,
                     ('wd', slot), 'A' if slot == 0 else 'B',
                     out_war_te=S.get(('wdtr_te', slot), 0))
            S[('wdsrc', 've')] = S[(('wd', slot), 'ladder_ve')]
            lad = S[(('wd', slot), 'ladder_ve')]
            src = wq_bf[:, qoff:qoff + HW]
            wtr_evac(src, 0, 8,
                     wdT[:, hh * 11:hh * 11 + 8, r * 128:(r + 1) * 128],
                     lad, S.get(('wevac_ve',), 0))
            t2 = wtr_evac(src, 8, 3,
                          wdT[:, hh * 11 + 8:hh * 11 + 11, r * 128:(r + 1) * 128],
                          lad, S.get(('wevac_ve',), 0))
            S[('wdtr_te', slot)] = t2
            pendd = None

        with nc.sbuf_tensor(pfx + "wstage", [128, HW], F32) as wstage:
            for rr in range(16):
                r, hh = divmod(rr, 2)
                slot = rr % 2
                qoff = slot * HW
                flush_wd()

                @block.sync
                def _(sync):
                    wait(sync, ctx.sVE, S.get(('wdsrc', 've'), 0))
                    wait(sync, ctx.sSC, S.get(('wdsrc', 'sc'), 0))
                    ctx.sIN.inc(sync.dma_start(
                        out=wstage[:],
                        in_=wd_d[r * 128:(r + 1) * 128, hh * HW:(hh + 1) * HW]), 16)
                ld = ctx.sIN.v
                quant_p1(wstage[:], HW, qoff, slot * 88,
                         ('wd', slot), 'A' if slot == 0 else 'B',
                         src_wait=(ctx.sIN, ld))
                S[('wdsrc', 'sc')] = S[(('wd', slot), 'r12_sc')]
                pendd = (slot, qoff, r, hh)
            flush_wd()
        wd_bar = S[('wevac_ve',)]
        wq_cm.__exit__(None, None, None)

        if dbg is not None:
            @block.sync
            def _(sync):
                wait(sync, ctx.sVE)
                ctx.sIN.inc(sync.dma_start(out=dbg['WGT'], in_=wgT[:]), 16)
                ctx.sIN.inc(sync.dma_start(out=dbg['WUT'], in_=wuT[:]), 16)
                ctx.sIN.inc(sync.dma_start(out=dbg['WDT'], in_=wdT[:]), 16)
            _dbg_ld = ctx.sIN.v

            @block.scalar
            def _(s):
                wait(s, ctx.sIN, _dbg_ld)

        for key in (('h', 0), ('h', 1), ('xq',)):
            S[(key, 'ladder_ve')] = ctx.sVE.v
            S[(key, 'rm_ve')] = ctx.sVE.v
            S[(key, 'r12_sc')] = ctx.sSC.v
        S[('evac_ve', 0)] = S[('evac_ve', 1)] = wd_bar

        if dbg is not None:
            dbg['G1SB'] = st.enter_context(nc.sbuf_tensor(pfx + "dbgsb", [128, 512], F32))
        xq_bf = st.enter_context(nc.sbuf_tensor(pfx + "xq_bf", [128, 1024], BF16))
        xqT0 = st.enter_context(nc.sbuf_tensor(pfx + "xqT", [128, 8, 128], BF16))
        hbuf = st.enter_context(nc.sbuf_tensor(pfx + "hbuf", [128, 512], F32))
        hq_bf = st.enter_context(nc.sbuf_tensor(pfx + "hq_bf", [128, 512], BF16))
        xqT = [xqT0[:],
               qb["A"][:, 2048:2560].bitcast(BF16).rearrange("p (j t) -> p j t", j=8)]
        hqs = [qb["Y"][:, 2560:2816].bitcast(BF16), hq_bf[:, 0:512]]
        h2s = [qb["Z"][:, 2048:2560], hbuf[:, 0:512]]
        hqT = [qb["Y"][:, 2048:2304].bitcast(BF16).rearrange("p (j t) -> p j t", j=4),
               qb["Y"][:, 2304:2560].bitcast(BF16).rearrange("p (j t) -> p j t", j=4)]

        CW = [512] * 5 + [256]
        IC0 = [0, 512, 1024, 1536, 2048, 2560]

        def emit_xdma(g):
            slot = g % 2

            @block.sync
            def _(sync):
                wait(sync, ctx.sVE, S.get(('xin', slot, 've'), 0))
                wait(sync, ctx.sSC, S.get(('xin', slot, 'sc'), 0))
                ctx.sIN.inc(sync.dma_start(out=ctx.xin[:, slot, :],
                                           in_=x3_d[g * 128:(g + 1) * 128, :]), 16)
            S[('xdma', g % 2)] = ctx.sIN.v

        def emit_xquant1(g):
            slot = g % 2
            quant_p1(ctx.xin[:, slot, :], 1024, 0, 0, ('xq',), 'A',
                     src_wait=(ctx.sIN, S[('xdma', slot)]))
            S[('xin', slot, 'sc')] = S[(('xq',), 'r12_sc')]

        def emit_xquant2(g):
            quant_p2(xq_bf[:], 1024, 0, 0, ('xq',), 'A',
                     out_war_te=S.get(('xtr_te',), 0))
            S[('xin', g % 2, 've')] = S[(('xq',), 'ladder_ve')]

        def emit_xtr_evac(g):
            slot = g % 2
            if dbg is not None and 'XQ' in dbg:
                @block.sync
                def _(sync):
                    wait(sync, ctx.sVE, S[(('xq',), 'ladder_ve')])
                    ctx.sIN.inc(sync.dma_start(
                        out=dbg['XQ'][g * 128:(g + 1) * 128, :], in_=xq_bf[:]), 16)
                _ldx = ctx.sIN.v

                @block.vector
                def _(v):
                    wait(v, ctx.sIN, _ldx)

            @block.tensor
            def _(t):
                wait(t, ctx.sVE, S[(('xq',), 'ladder_ve')])
                wait(t, ctx.sVE, S[('evac_ve', 0)])
                wait(t, ctx.sVE, S[('evac_ve', 1)])
                wait(t, ctx.sIN, ctx.const_ready_in)
                for j in range(8):
                    i = t.transpose(out=ctx.p6[:, j * 128:(j + 1) * 128],
                                    in_=xq_bf[:, j * 128:(j + 1) * 128],
                                    identity=ctx.idbf[:])
                ctx.sPE.inc(i)
            S[('xtr_te',)] = ctx.sPE.v

            @block.vector
            def _(v):
                wait(v, ctx.sPE, S[('xtr_te',)])
                wait(v, ctx.sPE, S.get(('gu_last_te', slot), 0))
                ctx.sVE.inc(v.tensor_copy(
                    out=xqT[slot][:],
                    in_=ctx.p6[:].rearrange("p (j t) -> p j t", j=8)))
            S[('xevac_ve', slot)] = ctx.sVE.v
            S[('evac_ve', 0)] = S[('evac_ve', 1)] = ctx.sVE.v

        def emit_gu(g, ic):
            p = ic % 2
            pg0 = ctx.p0 if p == 0 else ctx.p2
            pg1 = ctx.p1 if p == 0 else ctx.p3
            cw, ic0 = CW[ic], IC0[ic]

            @block.tensor
            def _(t):
                if ic == 0:
                    wait(t, ctx.sVE, S[('xevac_ve', g % 2)])
                    if g == 0:
                        wait(t, ctx.sVE, wgwu_bar)
                wait(t, ctx.sSC, S.get(('silu_sc', p), 0))
                wait(t, ctx.sVE, S.get(('hmult_ve', p), 0))
                for k in range(8):
                    t.matmul(out=pg0[:, 0:cw], lhsT=xqT[g % 2][:, k, :],
                             rhs=wgT[:, k, ic0:ic0 + cw],
                             start=(k == 0), stop=(k == 7), skip_group_check=True)
                for k in range(8):
                    i = t.matmul(out=pg1[:, 0:cw], lhsT=xqT[g % 2][:, k, :],
                                 rhs=wuT[:, k, ic0:ic0 + cw],
                                 start=(k == 0), stop=(k == 7), skip_group_check=True)
                ctx.sPE.inc(i)
            S[('gu_te', p)] = ctx.sPE.v
            if ic == 5:
                S[('gu_last_te', g % 2)] = ctx.sPE.v

        def emit_silu_mult(g, ic):
            p = ic % 2
            pg0 = ctx.p0 if p == 0 else ctx.p2
            pg1 = ctx.p1 if p == 0 else ctx.p3
            cw = CW[ic]
            gu_te = S[('gu_te', p)]
            h2 = h2s[p][:, 0:cw]
            if dbg is not None and 'G1' in dbg and g == 1:
                @block.vector
                def _(v):
                    wait(v, ctx.sPE, gu_te)
                    ctx.sVE.inc(v.tensor_copy(out=dbg['G1SB'][:, 0:cw],
                                              in_=pg0[:, 0:cw]))
                _cp = ctx.sVE.v

                @block.sync
                def _(sync):
                    wait(sync, ctx.sVE, _cp)
                    ctx.sIN.inc(sync.dma_start(
                        out=dbg['G1'][:, IC0[ic]:IC0[ic] + cw],
                        in_=dbg['G1SB'][:, 0:cw]), 16)
                _ldg = ctx.sIN.v

                @block.vector
                def _(v):
                    wait(v, ctx.sIN, _ldg)

            @block.scalar
            def _(s):
                wait(s, ctx.sPE, gu_te)
                wait(s, ctx.sVE, S.get((('h', p), 'ladder_ve'), 0))
                s.activation(out=h2, in_=pg0[:, 0:cw], func=AF.Exp, scale=-1.0)
                s.activation(out=h2, in_=h2, func=AF.Ln, bias=ctx.cone[:])
                ctx.sSC.inc(s.activation(out=h2, in_=h2, func=AF.Exp, scale=-1.0))
            S[('silu_sc', p)] = ctx.sSC.v

            @block.vector
            def _(v):
                wait(v, ctx.sSC, S[('silu_sc', p)])
                v.tensor_tensor(out=h2, in0=pg0[:, 0:cw], in1=h2, op=ALU.mult)
                ctx.sVE.inc(v.tensor_tensor(out=h2, in0=pg1[:, 0:cw], in1=h2,
                                            op=ALU.mult))
            S[('hmult_ve', p)] = ctx.sVE.v

        def emit_quant_h1(g, ic):
            p = ic % 2
            cw = CW[ic]
            if dbg is not None and 'H21' in dbg and g == 1:
                @block.vector
                def _(v):
                    ctx.sVE.inc(v.tensor_copy(out=dbg['G1SB'][:, 0:cw],
                                              in_=h2s[p][:, 0:cw]))
                _cp = ctx.sVE.v

                @block.sync
                def _(sync):
                    wait(sync, ctx.sVE, _cp)
                    ctx.sIN.inc(sync.dma_start(
                        out=dbg['H21'][:, IC0[ic]:IC0[ic] + cw],
                        in_=dbg['G1SB'][:, 0:cw]), 16)
                _ldh = ctx.sIN.v

                @block.vector
                def _(v):
                    wait(v, ctx.sIN, _ldh)
            quant_p1(h2s[p][:, 0:cw], cw, 1024 + 512 * p, 64 + 32 * p, ('h', p),
                     'A' if p == 0 else 'B',
                     src_wait=(ctx.sVE, S[('hmult_ve', p)]))

        def emit_quant_h2(g, ic):
            p = ic % 2
            cw = CW[ic]
            quant_p2(hqs[p][:, 0:cw], cw, 1024 + 512 * p, 64 + 32 * p, ('h', p),
                     'A' if p == 0 else 'B',
                     out_war_te=S.get(('tr_te', p), 0))
            if dbg is not None and 'HQ1' in dbg and g == 1:
                lad = S[(('h', p), 'ladder_ve')]

                @block.sync
                def _(sync):
                    wait(sync, ctx.sVE, lad)
                    ctx.sIN.inc(sync.dma_start(
                        out=dbg['HQ1'][:, IC0[ic]:IC0[ic] + cw],
                        in_=hqs[p][:, 0:cw]), 16)
                _ld = ctx.sIN.v

                @block.vector
                def _(v):
                    wait(v, ctx.sIN, _ld)

        def emit_trdown(g, ic):
            p = ic % 2
            nkk = CW[ic] // 128

            @block.tensor
            def _(t):
                wait(t, ctx.sVE, S[(('h', p), 'ladder_ve')])
                wait(t, ctx.sVE, S[('evac_ve', p)])
                for kk in range(nkk):
                    i = t.transpose(out=ctx.p6[:, p * 512 + kk * 128:p * 512 + (kk + 1) * 128],
                                    in_=hqs[p][:, kk * 128:(kk + 1) * 128],
                                    identity=ctx.idbf[:])
                ctx.sPE.inc(i)
            tr_te = ctx.sPE.v
            S[('tr_te', p)] = tr_te

            @block.vector
            def _(v):
                wait(v, ctx.sPE, tr_te)
                wait(v, ctx.sPE, S.get(('down_te', p), 0))
                ctx.sVE.inc(v.tensor_copy(
                    out=hqT[p][:, 0:nkk, :],
                    in_=ctx.p6[:, p * 512:p * 512 + nkk * 128].rearrange(
                        "p (j t) -> p j t", j=nkk)))
            S[('evac_ve', p)] = ctx.sVE.v

            @block.tensor
            def _(t):
                wait(t, ctx.sVE, S[('evac_ve', p)])
                if ic == 0:
                    wait(t, ctx.sVE, S.get(('lnz_ve',), 0))
                    if g == 0:
                        wait(t, ctx.sVE, wd_bar)
                for kk in range(nkk):
                    for half in range(2):
                        i = t.matmul(out=ctx.p45[:, half * 512:(half + 1) * 512],
                                     lhsT=hqT[p][:, kk, :],
                                     rhs=wdT[:, ic * 4 + kk, half * 512:(half + 1) * 512],
                                     start=(ic == 0 and kk == 0),
                                     stop=(ic == 5 and kk == nkk - 1),
                                     skip_group_check=True)
                ctx.sPE.inc(i)
            S[('down_te', p)] = ctx.sPE.v

        emit_xdma(0)
        emit_xquant1(0)
        emit_xquant2(0)
        emit_xtr_evac(0)
        for g in range(NT):
            for ic in range(6):
                s_ = 6 * g + ic
                emit_gu(g, ic)
                if s_ >= 2:
                    g2, ic2 = divmod(s_ - 2, 6)
                    emit_trdown(g2, ic2)
                    if ic2 == 5:
                        emit_ln(ctx, block, ctx.p45[:], x3_d, g2, out_d, g2)
                        S[('lnz_ve',)] = ctx.sVE.v
                if s_ >= 1:
                    g1, ic1 = divmod(s_ - 1, 6)
                    emit_quant_h2(g1, ic1)
                if g + 1 < NT:
                    if ic == 2:
                        emit_xdma(g + 1)
                    elif ic == 3:
                        emit_xquant1(g + 1)
                    elif ic == 4:
                        emit_xquant2(g + 1)
                emit_silu_mult(g, ic)
                emit_quant_h1(g, ic)
                if g + 1 < NT and ic == 5:
                    emit_xtr_evac(g + 1)
        g1, ic1 = divmod(6 * NT - 1, 6)
        emit_quant_h2(g1, ic1)
        for s_ in (6 * NT - 2, 6 * NT - 1):
            g2, ic2 = divmod(s_, 6)
            emit_trdown(g2, ic2)
            if ic2 == 5:
                emit_ln(ctx, block, ctx.p45[:], x3_d, g2, out_d, g2)


        emit_final_wait(ctx, block)
    return nc


from concourse.bass_utils import run_bass_kernel_spmd

_CONSTS = None
_FUSED = None


def _consts():
    global _CONSTS
    if _CONSTS is None:
        _CONSTS = host_consts()
    return _CONSTS


def _fused_prog():
    global _FUSED
    if _FUSED is not None:
        return _FUSED
    nc = bass.Bass("TRN2", target_bir_lowering=False)
    x0 = nc.dram_tensor("X0", [SEQ, DM], F32, kind="ExternalInput").ap()
    wq = nc.dram_tensor("WQ", [1024, 512], F32, kind="ExternalInput").ap()
    wkv = nc.dram_tensor("WKV", [1024, 1024], F32, kind="ExternalInput").ap()
    wo = nc.dram_tensor("WO", [512, 1024], F32, kind="ExternalInput").ap()
    bw = nc.dram_tensor("BW", [1024, 1024], F32, kind="ExternalInput").ap()
    bb = nc.dram_tensor("BB", [1024], F32, kind="ExternalInput").ap()
    la = nc.dram_tensor("LA", [1024, 8], F32, kind="ExternalInput").ap()
    lb = nc.dram_tensor("LB", [8, 8, 1024], F32, kind="ExternalInput").ap()
    rw = nc.dram_tensor("RW", [8, 1024], F32, kind="ExternalInput").ap()
    rb = nc.dram_tensor("RB", [8], F32, kind="ExternalInput").ap()
    wqkv = nc.dram_tensor("WQKV", [1024, 3072], F32, kind="ExternalInput").ap()
    wo1 = nc.dram_tensor("WO1", [1024, 1024], F32, kind="ExternalInput").ap()
    wg = nc.dram_tensor("WG", [2816, 1024], F32, kind="ExternalInput").ap()
    wu = nc.dram_tensor("WU", [2816, 1024], F32, kind="ExternalInput").ap()
    wd = nc.dram_tensor("WD", [1024, 2816], F32, kind="ExternalInput").ap()
    cdram = {
        "IDBF": nc.dram_tensor("IDBF", [128, 128], BF16, kind="ExternalInput").ap(),
        "IDF32": nc.dram_tensor("IDF32", [128, 128], F32, kind="ExternalInput").ap(),
        "HSEL": nc.dram_tensor("HSEL", [128, 2], F32, kind="ExternalInput").ap(),
        "HSELT": nc.dram_tensor("HSELT", [2, 128], F32, kind="ExternalInput").ap(),
        "TRIT": nc.dram_tensor("TRIT", [128, 128], BF16, kind="ExternalInput").ap(),
        "ROPE_COS": nc.dram_tensor("ROPE_COS", [128, NT, 32], F32, kind="ExternalInput").ap(),
        "ROPE_SIN": nc.dram_tensor("ROPE_SIN", [128, NT, 32], F32, kind="ExternalInput").ap(),
    }
    out = nc.dram_tensor("OUT", [SEQ, DM], F32, kind="ExternalOutput").ap()
    x1 = nc.dram_tensor("X1i", [SEQ, DM], F32, kind="Internal").ap()
    x2 = nc.dram_tensor("X2i", [SEQ, DM], F32, kind="Internal").ap()
    x3 = nc.dram_tensor("X3i", [SEQ, DM], F32, kind="Internal").ap()
    qs = nc.dram_tensor("QSPILL", [128, 8, SEQ], BF16, kind="Internal").ap()
    ks = nc.dram_tensor("KSPILL", [128, 8, SEQ], BF16, kind="Internal").ap()
    build_p0(nc, x0, wq, wkv, wo, x1, pfx="a_", cdram=cdram)
    build_p1(nc, x1, bw, bb, la, lb, rw, rb, x2, pfx="b_", cdram=cdram)
    build_p2(nc, x2, wqkv, wo1, x3, qs, ks, pfx="c_", cdram=cdram)
    build_p3(nc, x3, wg, wu, wd, out, pfx="d_", cdram=cdram)
    _FUSED = nc
    return nc


def kernel(L0_attn_x, L0_attn_Wq, L0_attn_Wkv, L0_attn_Wo, L0_ffn_base_w,
           L0_ffn_base_b, L0_ffn_lora_A, L0_ffn_lora_B, L0_ffn_router_w,
           L0_ffn_router_b, L1_attn_wqkv, L1_attn_wo, L1_ffn_w_gate,
           L1_ffn_w_up, L1_ffn_w_down):
    NCORES = 8
    c = _consts()
    x = np.ascontiguousarray(np.asarray(L0_attn_x, dtype=np.float32))
    base = {"WQ": L0_attn_Wq, "WKV": L0_attn_Wkv, "WO": L0_attn_Wo,
            "BW": L0_ffn_base_w, "BB": L0_ffn_base_b, "LA": L0_ffn_lora_A,
            "LB": L0_ffn_lora_B, "RW": L0_ffn_router_w, "RB": L0_ffn_router_b,
            "WQKV": L1_attn_wqkv, "WO1": L1_attn_wo, "WG": L1_ffn_w_gate,
            "WU": L1_ffn_w_up, "WD": L1_ffn_w_down,
            "IDBF": c["IDBF"], "IDF32": c["IDF32"], "HSEL": c["HSEL"],
            "HSELT": c["HSELT"], "TRIT": c["TRIT"],
            "ROPE_COS": c["ROPE_COS"], "ROPE_SIN": c["ROPE_SIN"]}
    res = run_bass_kernel_spmd(_fused_prog(),
                               [dict(base, X0=x[b]) for b in range(NCORES)],
                               core_ids=list(range(NCORES)))
    out = np.stack([np.asarray(res.results[b]["OUT"]) for b in range(NCORES)], axis=0)
    return out.astype(np.float32)
